# version 05
# baseline (speedup 1.0000x reference)
import numpy as np
import ml_dtypes
from contextlib import ExitStack

import concourse.bass as bass
import concourse.mybir as mybir
from concourse.bass_utils import run_bass_kernel_spmd

F32 = mybir.dt.float32
BF16 = mybir.dt.bfloat16
AF = mybir.ActivationFunctionType
ALU = mybir.AluOpType
AX = mybir.AxisListType

B, S, D = 4, 4096, 2048
T = 2048
NC_D = 16
INTER = 8192
NC_I = 64
H, QK, VD, QR, KVR, ROPE = 16, 128, 128, 682, 256, 64
QRP = 768
NC_QR = 6
KVAP = 384
THETA = 128000.0
SCALE_ATT = float(1.0 / np.sqrt(128.0))
N_TQB = 4
PHASE_LIMIT = None
DEBUG_Z = False
DEBUG_W = False


class Prog:
    def __init__(self, nc, stack):
        self.nc = nc
        self.stack = stack
        self.eng = {"pe": nc.tensor, "act": nc.scalar, "dve": nc.vector,
                    "pool": nc.gpsimd, "sp": nc.sync}
        self.sems = {}
        self.cnt = {}
        self.waited = {}

    def sem(self, name):
        if name not in self.sems:
            self.sems[name] = self.stack.enter_context(self.nc.semaphore(name))
            self.cnt[name] = 0
        return self.sems[name]

    def inc(self, inst, sname, n=1):
        h = self.sem(sname)
        self.cnt[sname] += n
        inst.then_inc(h, n)
        return (sname, self.cnt[sname])

    def wait(self, ename, *tokens):
        e = self.eng[ename]
        for t in tokens:
            if t is None:
                continue
            s, v = t
            key = (ename, s)
            if v > self.waited.get(key, 0):
                e.wait_ge(self.sems[s], v)
                self.waited[key] = v
                if ename == "pe":
                    self.pe_stall = True

    def guard(self, bank, lhsT, rhs):
        if getattr(self, "pe_stall", False):
            for _ in range(4):
                self.nc.tensor.matmul(bank, lhsT, rhs, start=True, stop=True)
            self.pe_stall = False

    def barrier(self):
        toks = [(s, c) for s, c in self.cnt.items() if c > 0]
        for e in ("pe", "act", "dve", "pool", "sp"):
            self.wait(e, *toks)

    def dma(self, qname, out, in_, sname, deps=()):
        self.wait(qname, *deps)
        inst = self.eng[qname].dma_start(out, in_)
        return self.inc(inst, sname, 16)


def build_program():
    nc = bass.Bass("TRN2", target_bir_lowering=False)
    stack = ExitStack()
    p = Prog(nc, stack)

    def dram(name, shape, dt, kind="ExternalInput"):
        return nc.dram_tensor(name, shape, dt, kind=kind).ap()

    xT = dram("xT", [NC_D, 128, T], F32)
    wup_b = dram("wup_b", [NC_I, 128, NC_D, 128], F32)
    wgt_b = dram("wgt_b", [NC_I, 128, NC_D, 128], F32)
    wdn_b = dram("wdn_b", [NC_I, 128, D], F32)
    qa_b = dram("qa_b", [NC_QR, 128, NC_D, 128], F32)
    qb_b = dram("qb_b", [H, 3, NC_QR, 128, 64], F32)
    kva_b = dram("kva_b", [3, 128, NC_D, 128], F32)
    kvar_b = dram("kvar_b", [NC_D, 128, 64], F32)
    kvb_v = dram("kvb_v", [H, 128, 2, 128], F32)
    kvb_k = dram("kvb_k", [H, 128, 2, 64], F32)
    o_b = dram("o_b", [NC_D, H, 128, 128], F32)
    su_ug = dram("su_ug", [NC_D, 128], F32)
    su_dp = dram("su_dp", [NC_I, 128], F32)
    ws_up = dram("ws_up", [NC_I, 128], F32)
    ws_gt = dram("ws_gt", [NC_I, 128], F32)
    ws_dp = dram("ws_dp", [NC_D, 128], F32)
    qaln = dram("qaln", [NC_QR, 128], F32)
    kvln = dram("kvln", [2, 128], F32)
    h128c = dram("h128c", [128, 128], BF16)
    hd64c = dram("hd64c", [128, 128], BF16)
    onescol = dram("onescol", [128, 1], BF16)
    onesrow = dram("onesrow", [1, 128], F32)
    identc = dram("identc", [128, 128], F32)
    cosq = dram("cosq", [64, T], BF16)
    sinq = dram("sinq", [64, T], BF16)

    yout = dram("yout", [NC_D, 128, T], F32, kind="ExternalOutput")

    z1d = dram("z1d", [NC_I, 128, T], BF16, kind="Internal")
    x1d = dram("x1d", [NC_D, 128, T], F32, kind="Internal")
    x2d = dram("x2d", [NC_D, 128, T], F32, kind="Internal")
    ccs = dram("ccs", [KVR + ROPE, T], BF16, kind="Internal")
    ccd = dram("ccd", [2 * (KVR + ROPE), T], BF16, kind="Internal")
    if DEBUG_Z:
        zdbg = dram("zdbg", [NC_I, 128, T], BF16, kind="ExternalOutput")
    if DEBUG_W:
        wdbg = dram("wdbg", [128, NC_I * 1024], BF16, kind="ExternalOutput")

    sb = lambda name, shape, dt: stack.enter_context(nc.sbuf_tensor(name, shape, dt))

    c_h128 = sb("c_h128", [128, 128], BF16)
    c_hd64 = sb("c_hd64", [128, 128], BF16)
    c_ones = sb("c_ones", [128, 1], BF16)
    c_orow = sb("c_orow", [1, 128], F32)
    c_id = sb("c_id", [128, 128], F32)
    pp = sb("pp", [128, 256], F32)
    PP_SU, PP_SUP, PP_SGT, PP_WDP, PP_QALN, PP_KVLN = 0, 16, 80, 144, 160, 166

    rowA = sb("rowA", [1, T], F32)
    rowB = sb("rowB", [1, T], F32)

    stg = [sb(f"stg{i}", [128, 2048], F32) for i in range(2)]
    xres = [sb(f"xres{i}", [128, 512], F32) for i in range(2)]
    oout = [sb(f"oout{i}", [128, 512], F32) for i in range(2)]

    tok = {}
    last_store = [None]
    stg_free = [(), ()]
    stg_idx = [0]

    def stg_next():
        si = stg_idx[0] % 2
        stg_idx[0] += 1
        return si

    def wload(dst_ap, src_ap, width, deps=()):
        si = stg_next()
        t1 = p.dma("sp", stg[si][:, 0:width], src_ap, "sW", deps=stg_free[si])
        p.wait("pool", t1, *[d for d in deps if d is not None])
        i = nc.gpsimd.tensor_copy(dst_ap, stg[si][:, 0:width])
        tk = p.inc(i, "sPOOL")
        stg_free[si] = (tk,)
        return tk

    def emit_scale_prep():
        with ExitStack() as lst:
            pst = lst.enter_context(nc.psum_tensor("pst", [128, 128], F32))
            tmp = lst.enter_context(nc.sbuf_tensor("tmp_sc", [128, 128], F32))
            tmp2 = lst.enter_context(nc.sbuf_tensor("tmp_sc2", [128, 128], F32))
            p.dma("sp", c_h128[:, :], h128c[:, :], "sX")
            p.dma("sp", c_hd64[:, :], hd64c[:, :], "sX")
            p.dma("sp", c_ones[:, :], onescol[:, :], "sX")
            p.dma("sp", c_orow[:, :], onesrow[:, :], "sX")
            t5 = p.dma("sp", c_id[:, :], identc[:, :], "sX")
            tok["consts"] = t5

            def tr_to_pp(src_ap, n, col0, scale, pre=None):
                tload = p.dma("sp", tmp[0:n, :], src_ap, "sX",
                              deps=(tok.get("pst_free"),))
                ttok = tload
                if pre is not None:
                    tload2 = p.dma("sp", tmp2[0:n, :], pre, "sX")
                    p.wait("dve", tload, tload2)
                    i = nc.vector.tensor_mul(tmp[0:n, :], tmp[0:n, :], tmp2[0:n, :])
                    ttok = p.inc(i, "sDVE")
                p.wait("pe", ttok, t5, tok.get("pst_free"))
                i = nc.tensor.transpose(pst[:, 0:n], tmp[0:n, :], c_id[0:n, 0:n])
                mtok = p.inc(i, "sPE")
                p.wait("act", mtok)
                i = nc.scalar.activation(pp[:, col0:col0 + n], pst[:, 0:n], AF.Identity,
                                         bias=0.0, scale=float(scale))
                etok = p.inc(i, "sACT")
                tok["pst_free"] = etok
                return etok

            tr_to_pp(su_ug[:, :], NC_D, PP_SU, 1.0 / (64.0 * np.sqrt(2048.0)))
            tr_to_pp(ws_up[:, :], NC_I, PP_SUP, 1.0 / np.sqrt(8192.0), pre=su_dp[:, :])
            tr_to_pp(ws_gt[:, :], NC_I, PP_SGT, 64.0)
            tr_to_pp(ws_dp[:, :], NC_D, PP_WDP, 32.0)
            tr_to_pp(qaln[:, :], NC_QR, PP_QALN, 1.0)
            tok["pp_done"] = tr_to_pp(kvln[:, :], 2, PP_KVLN, 1.0)

    def emit_ln_stats(src_chunks, eps, tag, mu_bc, al_bc, xb16, sqb):
        with ExitStack() as lst:
            psx = lst.enter_context(nc.psum_tensor("psx" + tag, [1, T], F32))
            psx2 = lst.enter_context(nc.psum_tensor("psx2" + tag, [1, T], F32))
            for a in range(NC_D):
                si = stg_next()
                xf = stg[si]
                xb = xb16[a % 2]
                sq = sqb[a % 2]
                lt = p.dma("sp", xf[:, :], src_chunks(a), "sX", deps=stg_free[si])
                p.wait("pool", lt, tok.get(f"{tag}_mmdone_{a-2}"))
                i = nc.gpsimd.tensor_copy(xb[:, :], xf[:, :])
                ct = p.inc(i, "sPOOL")
                tok[f"{tag}_cast_{a}"] = ct
                p.wait("act", lt, tok.get(f"{tag}_sqmm_{a-2}"))
                i = nc.scalar.activation(sq[:, :], xf[:, :], AF.Square)
                sqt = p.inc(i, "sACT")
                tok[f"{tag}_sq_{a}"] = sqt
                stg_free[si] = (ct, sqt)
                p.wait("pe", ct, sqt)
                if a == 0:
                    p.guard(psx[0:1, 0:512], c_ones[:, :], xb[:, 0:512])
                for tq in range(N_TQB):
                    i = nc.tensor.matmul(psx[0:1, tq * 512:(tq + 1) * 512], c_ones[:, :],
                                         xb[:, tq * 512:(tq + 1) * 512],
                                         start=(a == 0), stop=(a == NC_D - 1))
                tok[f"{tag}_mmdone_{a}"] = p.inc(i, "sPE")
                for tq in range(N_TQB):
                    i = nc.tensor.matmul(psx2[0:1, tq * 512:(tq + 1) * 512], c_ones[:, :],
                                         sq[:, tq * 512:(tq + 1) * 512],
                                         start=(a == 0), stop=(a == NC_D - 1))
                tok[f"{tag}_sqmm_{a}"] = p.inc(i, "sPE")
            mmx = tok[f"{tag}_mmdone_{NC_D-1}"]
            mmx2 = tok[f"{tag}_sqmm_{NC_D-1}"]
            rowS = mu_bc[0:1, :]
            p.wait("dve", mmx, mmx2)
            i = nc.vector.tensor_scalar_mul(rowA[:, :], psx[0:1, :], 1.0 / 2048.0)
            t = p.inc(i, "sDVE")
            p.wait("dve", t)
            i = nc.vector.tensor_scalar_mul(rowS, psx2[0:1, :], 1.0 / 2048.0)
            t = p.inc(i, "sDVE")
            p.wait("dve", t)
            i = nc.vector.tensor_mul(rowB[:, :], rowA[:, :], rowA[:, :])
            t = p.inc(i, "sDVE")
            p.wait("dve", t)
            i = nc.vector.tensor_sub(rowS, rowS, rowB[:, :])
            trow = p.inc(i, "sDVE")
            p.wait("dve", trow)
            i = nc.vector.tensor_scalar_add(rowS, rowS, float(eps))
            trow = p.inc(i, "sDVE")
            p.wait("act", trow)
            i = nc.scalar.activation(rowB[:, :], rowS, AF.Ln)
            t = p.inc(i, "sACT")
            p.wait("act", t)
            i = nc.scalar.activation(rowB[:, :], rowB[:, :], AF.Exp, scale=-0.5)
            ta = p.inc(i, "sACT")
        with ExitStack() as lst:
            bcp = lst.enter_context(nc.psum_tensor("bcps" + tag, [128, 512], F32))
            last = None
            for (row, dst) in ((rowA, mu_bc), (rowB, al_bc)):
                for tq in range(N_TQB):
                    p.wait("pe", ta, tok.get(f"{tag}bcp_free"))
                    i = nc.tensor.matmul(bcp[:, :], c_orow[:, :],
                                         row[0:1, tq * 512:(tq + 1) * 512],
                                         start=True, stop=True)
                    mt = p.inc(i, "sPE")
                    p.wait("act", mt)
                    i = nc.scalar.copy(dst[:, tq * 512:(tq + 1) * 512], bcp[:, :])
                    et = p.inc(i, "sACT")
                    tok[f"{tag}bcp_free"] = et
                    last = et
        return (last,)

    def emit_ffn(passidx, src_dram, res_dram, out_dram):
        tg = f"f{passidx}"
        p.barrier()
        with ExitStack() as lst:
            lsb = lambda name, shape, dt: lst.enter_context(
                nc.sbuf_tensor(name + tg, shape, dt))
            mu_bc = lsb("mu_bc", [128, T], F32)
            al_bc = lsb("al_bc", [128, T], F32)
            xb16 = [lsb(f"xb{i}", [128, T], BF16) for i in range(2)]
            sqb = [lsb(f"sq{i}", [128, T], BF16) for i in range(2)]
            ln_tok = emit_ln_stats(lambda a: src_dram[a, :, :], 1e-5, tg + "ln",
                                   mu_bc, al_bc, xb16, sqb)
            v = [lsb(f"v{a}", [128, T], BF16) for a in range(NC_D)]
            for a in range(NC_D):
                si = stg_next()
                buf = stg[si]
                lt = p.dma("sp", buf[:, :], src_dram[a, :, :], "sX",
                           deps=stg_free[si])
                p.wait("dve", lt, *ln_tok)
                p.wait("dve", tok.get(f"{tg}v_done_{a-2}"))
                i = nc.vector.tensor_sub(v[a][:, :], buf[:, :], mu_bc[:, :])
                t = p.inc(i, "sDVE")
                stg_free[si] = (t,)
                p.wait("dve", t)
                i = nc.vector.scalar_tensor_tensor(
                    v[a][:, :], v[a][:, :], pp[:, PP_SU + a:PP_SU + a + 1], al_bc[:, :],
                    ALU.mult, ALU.mult)
                tok[f"{tg}v_done_{a}"] = p.inc(i, "sDVE")
            bf_prev = [tok[f"{tg}v_done_{a}"] for a in range(NC_D)]
            for st_i in range(4):
                bit = 1 << st_i
                pairs = [(a, a | bit) for a in range(NC_D) if not (a & bit)]
                res_t = {}
                for k, (a, b_) in enumerate(pairs):
                    en = "pool" if k % 2 == 0 else "dve"
                    sn = "sPOOL" if k % 2 == 0 else "sDVE"
                    p.wait(en, bf_prev[a], bf_prev[b_])
                    i = p.eng[en].tensor_add(v[a][:, :], v[a][:, :], v[b_][:, :])
                    ta_ = p.inc(i, sn)
                    p.wait("dve", ta_)
                    i = nc.vector.scalar_tensor_tensor(v[b_][:, :], v[b_][:, :], -2.0,
                                                       v[a][:, :], ALU.mult, ALU.add)
                    tb_ = p.inc(i, "sDVE")
                    res_t[a] = tb_
                    res_t[b_] = tb_
                for a2, t2 in res_t.items():
                    bf_prev[a2] = t2
            with ExitStack() as pl:
                psu = [pl.enter_context(nc.psum_tensor(f"psu{tg}{i}", [128, 512], F32))
                       for i in range(2)]
                for a in range(NC_D):
                    for tq in range(N_TQB):
                        ui = a * N_TQB + tq
                        bank = psu[ui % 2]
                        p.wait("pe", bf_prev[a], tok.get(f"{tg}ue_{ui-2}"),
                               tok.get("consts"))
                        i = nc.tensor.matmul(bank[:, :], c_h128[:, :],
                                             v[a][:, tq * 512:(tq + 1) * 512],
                                             start=True, stop=True)
                        mt = p.inc(i, "sPE")
                        p.wait("act", mt)
                        i = nc.scalar.copy(v[a][:, tq * 512:(tq + 1) * 512], bank[:, :])
                        tok[f"{tg}ue_{ui}"] = p.inc(i, "sACT")
            u_last = tok[f"{tg}ue_{NC_D * N_TQB - 1}"]

            wub = [lsb(f"wub{i}", [128, NC_D * 128], BF16) for i in range(2)]
            wgb = [lsb(f"wgb{i}", [128, NC_D * 128], BF16) for i in range(2)]
            gbuf = [lsb(f"gb{i}", [128, 512], BF16) for i in range(2)]
            hbuf = [lsb(f"hb{i}", [128, 512], BF16) for i in range(2)]
            z1sb = [lsb(f"z1sb{i}", [128, T], BF16) for i in range(2)]
            with ExitStack() as pl:
                psU = [pl.enter_context(nc.psum_tensor(f"psU{tg}{i}", [128, 512], F32))
                       for i in range(2)]
                psG = [pl.enter_context(nc.psum_tensor(f"psG{tg}{i}", [128, 512], F32))
                       for i in range(2)]
                psZ = [pl.enter_context(nc.psum_tensor(f"psZ{tg}{i}", [128, 512], F32))
                       for i in range(2)]
                for c in range(NC_I):
                    wu = wub[c % 2]
                    wg = wgb[c % 2]
                    wlt = wload(wu[:, :], wup_b[c, :, :, :].rearrange("p k i -> p (k i)"),
                                NC_D * 128, deps=(tok.get(f"{tg}upmm_{c-2}"),))
                    wlt2 = wload(wg[:, :], wgt_b[c, :, :, :].rearrange("p k i -> p (k i)"),
                                 NC_D * 128, deps=(tok.get(f"{tg}gtmm_{c-2}"),))
                    zs = z1sb[c % 2]
                    for tq in range(N_TQB):
                        un = c * N_TQB + tq
                        bu, bg, bz = psU[un % 2], psG[un % 2], psZ[un % 2]
                        p.wait("pe", wlt, u_last, tok.get(f"{tg}hh_{un-2}"))
                        p.guard(bu[:, :], wu[:, 0:128], v[0][:, tq * 512:(tq + 1) * 512])
                        for k in range(NC_D):
                            i = nc.tensor.matmul(bu[:, :], wu[:, k * 128:(k + 1) * 128],
                                                 v[k][:, tq * 512:(tq + 1) * 512],
                                                 start=(k == 0), stop=(k == NC_D - 1))
                        tok_up = p.inc(i, "sPE")
                        if tq == N_TQB - 1:
                            tok[f"{tg}upmm_{c}"] = tok_up
                        p.wait("pe", wlt2, tok.get(f"{tg}sil_{un-2}"))
                        p.guard(bg[:, :], wg[:, 0:128], v[0][:, tq * 512:(tq + 1) * 512])
                        for k in range(NC_D):
                            i = nc.tensor.matmul(bg[:, :], wg[:, k * 128:(k + 1) * 128],
                                                 v[k][:, tq * 512:(tq + 1) * 512],
                                                 start=(k == 0), stop=(k == NC_D - 1))
                        tok_gt = p.inc(i, "sPE")
                        if tq == N_TQB - 1:
                            tok[f"{tg}gtmm_{c}"] = tok_gt
                        g = gbuf[un % 2]
                        hh = hbuf[un % 2]
                        p.wait("act", tok_gt, tok.get(f"{tg}hh_{un-2}"))
                        i = nc.scalar.activation(g[:, :], bg[:, :], AF.Silu, bias=0.0,
                                                 scale=pp[:, PP_SGT + c:PP_SGT + c + 1])
                        sil = p.inc(i, "sACT")
                        tok[f"{tg}sil_{un}"] = sil
                        p.wait("dve", sil, tok_up, tok.get(f"{tg}zmm_{un-2}"))
                        i = nc.vector.scalar_tensor_tensor(
                            hh[:, :], bu[:, :], pp[:, PP_SUP + c:PP_SUP + c + 1], g[:, :],
                            ALU.mult, ALU.mult)
                        hht = p.inc(i, "sDVE")
                        tok[f"{tg}hh_{un}"] = hht
                        p.wait("pe", hht, tok.get(f"{tg}ze_{un-2}"))
                        i = nc.tensor.matmul(bz[:, :], c_h128[:, :], hh[:, :],
                                             start=True, stop=True)
                        zmt = p.inc(i, "sPE")
                        tok[f"{tg}zmm_{un}"] = zmt
                        p.wait("act", zmt, tok.get(f"{tg}zw_{c-2}"))
                        i = nc.scalar.copy(zs[:, tq * 512:(tq + 1) * 512], bz[:, :])
                        tok[f"{tg}ze_{un}"] = p.inc(i, "sACT")
                    tok[f"{tg}zw_{c}"] = p.dma(
                        "sp", z1d[c, :, :], zs[:, :], "sS",
                        deps=(tok[f"{tg}ze_{c * N_TQB + N_TQB - 1}"],))
            spill_done = tok[f"{tg}zw_{NC_I - 1}"]

        p.barrier()
        with ExitStack() as lst:
            lsb = lambda name, shape, dt: lst.enter_context(
                nc.sbuf_tensor(name + tg, shape, dt))
            wstrip = lsb("wstrip", [128, NC_I * 1024], BF16)
            wdl = [lsb(f"wdl{i}", [128, 1024], BF16) for i in range(2)]
            ztile = [lsb(f"zt{i}", [128, 512], BF16) for i in range(2)]
            for rh in range(2):
                with ExitStack() as pl:
                    psW = [pl.enter_context(
                        nc.psum_tensor(f"psW{tg}{rh}{i}", [128, 512], F32))
                        for i in range(2)]
                    prev_mm = tok.get(f"{tg}dmm_{rh-1}_{N_TQB-1}_{NC_I-1}")
                    p.wait("pe", tok.get(f"{tg}de_{rh-1}_{N_TQB-1}"))
                    for m in range(NC_I):
                        wb = wdl[m % 2]
                        wlt = wload(wb[:, :], wdn_b[m, :, rh * 1024:(rh + 1) * 1024],
                                    1024, deps=(tok.get(f"{tg}wse_{rh}_{m-2}"),))
                        for j in range(2):
                            bank = psW[(2 * m + j) % 2]
                            p.wait("pe", wlt, tok.get(f"{tg}wse2_{rh}_{2*m+j-2}"))
                            i = nc.tensor.matmul(bank[:, :], c_hd64[:, :],
                                                 wb[:, j * 512:(j + 1) * 512],
                                                 start=True, stop=True)
                            mt = p.inc(i, "sPE")
                            p.wait("act", mt, prev_mm)
                            i = nc.scalar.copy(
                                wstrip[:, m * 1024 + j * 512:m * 1024 + (j + 1) * 512],
                                bank[:, :])
                            et = p.inc(i, "sACT")
                            tok[f"{tg}wse2_{rh}_{2*m+j}"] = et
                            if j == 1:
                                tok[f"{tg}wse_{rh}_{m}"] = et
                    wdone = tok[f"{tg}wse_{rh}_{NC_I - 1}"]
                    if DEBUG_W and rh == 0 and passidx == 0:
                        p.dma("sp", wdbg[:, :], wstrip[:, :], "sS", deps=(wdone,))
                with ExitStack() as pl:
                    psD = [pl.enter_context(
                        nc.psum_tensor(f"psD{tg}{rh}_{r}", [128, 512], F32))
                        for r in range(8)]
                    for tq in range(N_TQB):
                        for m in range(NC_I):
                            zt = ztile[m % 2]
                            zlt = p.dma("sp", zt[:, :],
                                        z1d[:, 2 * m:2 * m + 2, tq * 512:(tq + 1) * 512],
                                        "sX",
                                        deps=(spill_done,
                                              tok.get(f"{tg}dmm_{rh}_{tq}_{m-2}"),))
                            p.wait("pe", zlt, wdone)
                            if m == 0:
                                p.wait("pe", tok.get(f"{tg}de_{rh}_{tq-1}"))
                                p.guard(psD[0][:, :],
                                        wstrip[:, m * 1024:m * 1024 + 128], zt[:, :])
                            for r8 in range(8):
                                i = nc.tensor.matmul(
                                    psD[r8][:, :],
                                    wstrip[:, m * 1024 + r8 * 128:m * 1024 + (r8 + 1) * 128],
                                    zt[:, :], start=(m == 0), stop=(m == NC_I - 1))
                            tok[f"{tg}dmm_{rh}_{tq}_{m}"] = p.inc(i, "sPE")
                        dmt = tok[f"{tg}dmm_{rh}_{tq}_{NC_I - 1}"]
                        ev_last = None
                        for r8 in range(8):
                            r = rh * 8 + r8
                            xr = xres[r8 % 2]
                            rlt = p.dma("sp", xr[:, :],
                                        res_dram[r, :, tq * 512:(tq + 1) * 512], "sX",
                                        deps=(tok.get(f"{tg}dev_{rh}_{tq}_{r8-2}"),))
                            ob = oout[r8 % 2]
                            p.wait("dve", dmt, rlt, tok.get(f"{tg}dst_{rh}_{tq}_{r8-2}"))
                            i = nc.vector.scalar_tensor_tensor(
                                ob[:, :], psD[r8][:, :], pp[:, PP_WDP + r:PP_WDP + r + 1],
                                xr[:, :], ALU.mult, ALU.add)
                            et = p.inc(i, "sDVE")
                            tok[f"{tg}dev_{rh}_{tq}_{r8}"] = et
                            tok[f"{tg}dst_{rh}_{tq}_{r8}"] = p.dma(
                                "sp", out_dram[r, :, tq * 512:(tq + 1) * 512],
                                ob[:, :], "sS", deps=(et,))
                            ev_last = et
                        tok[f"{tg}de_{rh}_{tq}"] = ev_last
            last_store[0] = tok[f"{tg}dst_1_{N_TQB-1}_7"]

    def emit_attention():
        tg = "at"
        p.barrier()
        with ExitStack() as lst:
            lsb = lambda name, shape, dt: lst.enter_context(
                nc.sbuf_tensor(name + tg, shape, dt))
            qtil = [lsb(f"qtil{j}", [128, T], BF16) for j in range(NC_QR)]
            cosk = lsb("cosk", [64, T], BF16)
            sink = lsb("sink", [64, T], BF16)
            t1 = p.dma("sp", cosk[:, :], cosq[:, :], "sX")
            t2 = p.dma("sp", sink[:, :], sinq[:, :], "sX")
            sA = ExitStack()
            lsa = lambda name, shape, dt: sA.enter_context(
                nc.sbuf_tensor(name + tg, shape, dt))
            mu_bc = lsa("mu_bc", [128, T], F32)
            al_bc = lsa("al_bc", [128, T], F32)
            xb16 = [lsa(f"xb{i}", [128, T], BF16) for i in range(2)]
            sqb = [lsa(f"sq{i}", [128, T], BF16) for i in range(2)]
            ln_tok = emit_ln_stats(lambda a: x1d[a, :, :], 1e-5, tg + "ln",
                                   mu_bc, al_bc, xb16, sqb)
            y = [lsa(f"y{a}", [128, T], BF16) for a in range(NC_D)]
            for a in range(NC_D):
                si = stg_next()
                buf = stg[si]
                lt = p.dma("sp", buf[:, :], x1d[a, :, :], "sX",
                           deps=stg_free[si])
                p.wait("dve", lt, *ln_tok)
                p.wait("dve", tok.get(f"{tg}y_{a-2}"))
                i = nc.vector.tensor_sub(y[a][:, :], buf[:, :], mu_bc[:, :])
                t = p.inc(i, "sDVE")
                stg_free[si] = (t,)
                p.wait("dve", t)
                i = nc.vector.tensor_mul(y[a][:, :], y[a][:, :], al_bc[:, :])
                tok[f"{tg}y_{a}"] = p.inc(i, "sDVE")
            y_last = tok[f"{tg}y_{NC_D-1}"]

            kvw = [lsa(f"kvw{i}", [128, NC_D * 128], BF16) for i in range(2)]
            cf32 = [lsa(f"cf{j}", [128, T], BF16) for j in range(2)]
            kpe = lsa("kpe", [64, T], BF16)
            csq = lsa("csq", [128, T], BF16)
            ctil = cf32
            kro = lsa("kro", [64, T], BF16)
            ksc = lsa("ksc", [64, T], BF16)
            with ExitStack() as pl:
                psA = [pl.enter_context(nc.psum_tensor(f"psA{tg}{i}", [128, 512], F32))
                       for i in range(2)]
                psR = pl.enter_context(nc.psum_tensor(f"psR{tg}", [1, T], F32))
                bcp = pl.enter_context(nc.psum_tensor(f"bcpA{tg}", [128, 512], F32))
                for ic in range(3):
                    wb = kvw[ic % 2]
                    wlt = wload(wb[:, :], kva_b[ic, :, :, :].rearrange("p k i -> p (k i)"),
                                NC_D * 128, deps=(tok.get(f"{tg}kvmm_{ic-2}"),))
                    for tq in range(N_TQB):
                        un = ic * N_TQB + tq
                        bank = psA[un % 2]
                        p.wait("pe", wlt, y_last, tok.get(f"{tg}kve_{un-2}"))
                        p.guard(bank[:, :], wb[:, 0:128], y[0][:, tq * 512:(tq + 1) * 512])
                        for k in range(NC_D):
                            i = nc.tensor.matmul(bank[:, :], wb[:, k * 128:(k + 1) * 128],
                                                 y[k][:, tq * 512:(tq + 1) * 512],
                                                 start=(k == 0), stop=(k == NC_D - 1))
                        mt = p.inc(i, "sPE")
                        if tq == N_TQB - 1:
                            tok[f"{tg}kvmm_{ic}"] = mt
                        p.wait("act", mt)
                        if ic < 2:
                            i = nc.scalar.copy(cf32[ic][:, tq * 512:(tq + 1) * 512],
                                               bank[:, :])
                        else:
                            i = nc.scalar.copy(kpe[:, tq * 512:(tq + 1) * 512],
                                               bank[0:64, :])
                        tok[f"{tg}kve_{un}"] = p.inc(i, "sACT")
                for j in range(2):
                    p.wait("act", tok[f"{tg}kve_{j * N_TQB + N_TQB - 1}"])
                    i = nc.scalar.activation(csq[:, :], cf32[j][:, :], AF.Square)
                    sqt = p.inc(i, "sACT")
                    p.wait("pe", sqt)
                    for tq in range(N_TQB):
                        i = nc.tensor.matmul(psR[0:1, tq * 512:(tq + 1) * 512],
                                             c_ones[:, :],
                                             csq[:, tq * 512:(tq + 1) * 512],
                                             start=(j == 0), stop=(j == 1))
                    csqmm = p.inc(i, "sPE")
                p.wait("dve", csqmm)
                rowS = al_bc[0:1, :]
                i = nc.vector.tensor_scalar_mul(rowS, psR[0:1, :], 1.0 / 256.0)
                t = p.inc(i, "sDVE")
                p.wait("dve", t)
                i = nc.vector.tensor_scalar_add(rowS, rowS, 1e-6)
                t = p.inc(i, "sDVE")
                p.wait("act", t)
                i = nc.scalar.activation(rowB[:, :], rowS, AF.Ln)
                t = p.inc(i, "sACT")
                p.wait("act", t)
                i = nc.scalar.activation(rowB[:, :], rowB[:, :], AF.Exp, scale=-0.5)
                rt = p.inc(i, "sACT")
                for tq in range(N_TQB):
                    p.wait("pe", rt, tok.get(f"{tg}bcq_free"))
                    i = nc.tensor.matmul(bcp[:, :], c_orow[:, :],
                                         rowB[0:1, tq * 512:(tq + 1) * 512],
                                         start=True, stop=True)
                    mt = p.inc(i, "sPE")
                    p.wait("act", mt)
                    i = nc.scalar.copy(al_bc[:, tq * 512:(tq + 1) * 512], bcp[:, :])
                    tok[f"{tg}bcq_free"] = p.inc(i, "sACT")
                rbc_tok = tok[f"{tg}bcq_free"]
                ct_tok = []
                for j in range(2):
                    p.wait("dve", rbc_tok)
                    i = nc.vector.scalar_tensor_tensor(
                        ctil[j][:, :], ctil[j][:, :], pp[:, PP_KVLN + j:PP_KVLN + j + 1],
                        al_bc[:, :], ALU.mult, ALU.mult)
                    ct_tok.append(p.inc(i, "sDVE"))
                krw = kvw[0]
                krl = wload(krw[:, 0:NC_D * 64],
                            kvar_b[:, :, :].rearrange("k p i -> p k i"),
                            NC_D * 64, deps=(tok[f"{tg}kvmm_{2}"],))
                for tq in range(N_TQB):
                    bank = psA[tq % 2]
                    p.wait("pe", krl, tok.get(f"{tg}kre_{tq-2}"))
                    p.guard(bank[0:64, :], krw[:, 0:64], y[0][:, tq * 512:(tq + 1) * 512])
                    for k in range(NC_D):
                        i = nc.tensor.matmul(bank[0:64, :], krw[:, k * 64:(k + 1) * 64],
                                             y[k][:, tq * 512:(tq + 1) * 512],
                                             start=(k == 0), stop=(k == NC_D - 1))
                    mt = p.inc(i, "sPE")
                    if tq == N_TQB - 1:
                        tok[f"{tg}krmm"] = mt
                    p.wait("act", mt)
                    i = nc.scalar.copy(ksc[:, tq * 512:(tq + 1) * 512], bank[0:64, :])
                    tok[f"{tg}kre_{tq}"] = p.inc(i, "sACT")
                krot_tok = tok[f"{tg}kre_{N_TQB-1}"]
                p.wait("pool", t1, t2, krot_tok, tok[f"{tg}kve_{2 * N_TQB + N_TQB - 1}"])
                i = nc.gpsimd.tensor_mul(kro[:, :], kpe[:, :], cosk[:, :])
                ta = p.inc(i, "sPOOL")
                i = nc.gpsimd.tensor_mul(ksc[:, :], ksc[:, :], sink[:, :])
                tb = p.inc(i, "sPOOL")
                p.wait("pool", ta, tb)
                i = nc.gpsimd.tensor_add(kro[:, :], kro[:, :], ksc[:, :])
                kro_tok = p.inc(i, "sPOOL")
                pk1 = p.dma("sp", ccs[0:128, :], ctil[0][:, :], "sS", deps=(ct_tok[0],))
                pk2 = p.dma("sp", ccs[128:256, :], ctil[1][:, :], "sS", deps=(ct_tok[1],))
                pk3 = p.dma("sp", ccs[256:320, :], kro[:, :], "sS", deps=(kro_tok,))
                p.wait("pool", pk1, pk2, pk3)
                i = nc.gpsimd.collective_compute(
                    "AllGather", ALU.bypass,
                    replica_groups=[[0, 1], [2, 3], [4, 5], [6, 7]],
                    ins=[ccs.opt()], outs=[ccd.opt()])
                cc_tok = p.inc(i, "sCC")

                for ic in range(NC_QR):
                    wb = kvw[ic % 2]
                    wlt = wload(wb[:, :], qa_b[ic, :, :, :].rearrange("p k i -> p (k i)"),
                                NC_D * 128, deps=(tok.get(f"{tg}qamm_{ic-2}"),
                                                  tok.get(f"{tg}krmm") if ic == 0 else None,))
                    for tq in range(N_TQB):
                        un = ic * N_TQB + tq
                        bank = psA[un % 2]
                        p.wait("pe", wlt, tok.get(f"{tg}qae_{un-2}"), cc_tok)
                        p.guard(bank[:, :], wb[:, 0:128], y[0][:, tq * 512:(tq + 1) * 512])
                        for k in range(NC_D):
                            i = nc.tensor.matmul(bank[:, :], wb[:, k * 128:(k + 1) * 128],
                                                 y[k][:, tq * 512:(tq + 1) * 512],
                                                 start=(k == 0), stop=(k == NC_D - 1))
                        mt = p.inc(i, "sPE")
                        if tq == N_TQB - 1:
                            tok[f"{tg}qamm_{ic}"] = mt
                        p.wait("act", mt)
                        i = nc.scalar.copy(qtil[ic][:, tq * 512:(tq + 1) * 512], bank[:, :])
                        tok[f"{tg}qae_{un}"] = p.inc(i, "sACT")
                for j in range(NC_QR):
                    p.wait("act", tok[f"{tg}qae_{j * N_TQB + N_TQB - 1}"], csqmm)
                    i = nc.scalar.activation(csq[:, :], qtil[j][:, :], AF.Square)
                    sqt = p.inc(i, "sACT")
                    p.wait("pe", sqt)
                    for tq in range(N_TQB):
                        i = nc.tensor.matmul(psR[0:1, tq * 512:(tq + 1) * 512],
                                             c_ones[:, :],
                                             csq[:, tq * 512:(tq + 1) * 512],
                                             start=(j == 0), stop=(j == NC_QR - 1))
                    qsqmm = p.inc(i, "sPE")
                p.wait("dve", qsqmm, *ct_tok)
                rowS = al_bc[0:1, :]
                i = nc.vector.tensor_scalar_mul(rowS, psR[0:1, :], 1.0 / 682.0)
                t = p.inc(i, "sDVE")
                p.wait("dve", t)
                i = nc.vector.tensor_scalar_add(rowS, rowS, 1e-6)
                t = p.inc(i, "sDVE")
                p.wait("act", t)
                i = nc.scalar.activation(rowB[:, :], rowS, AF.Ln)
                t = p.inc(i, "sACT")
                p.wait("act", t)
                i = nc.scalar.activation(rowB[:, :], rowB[:, :], AF.Exp, scale=-0.5)
                rt = p.inc(i, "sACT")
                for tq in range(N_TQB):
                    p.wait("pe", rt, tok.get(f"{tg}bcq_free"), rbc_tok, *ct_tok)
                    i = nc.tensor.matmul(bcp[:, :], c_orow[:, :],
                                         rowB[0:1, tq * 512:(tq + 1) * 512],
                                         start=True, stop=True)
                    mt = p.inc(i, "sPE")
                    p.wait("act", mt)
                    i = nc.scalar.copy(al_bc[:, tq * 512:(tq + 1) * 512], bcp[:, :])
                    tok[f"{tg}bcq_free"] = p.inc(i, "sACT")
                qr_bc = tok[f"{tg}bcq_free"]
                qt_tok = []
                for j in range(NC_QR):
                    p.wait("dve", qr_bc)
                    i = nc.vector.scalar_tensor_tensor(
                        qtil[j][:, :], qtil[j][:, :], pp[:, PP_QALN + j:PP_QALN + j + 1],
                        al_bc[:, :], ALU.mult, ALU.mult)
                    qt_tok.append(p.inc(i, "sDVE"))

            sA.close()
            p.barrier()
            ctf = [lsb(f"ctf{j}", [128, 2 * T], BF16) for j in range(2)]
            gview = ccd.rearrange("(r f) t -> f r t", r=2)
            g1 = p.dma("sp", ctf[0][:, :], gview[0:128, :, :], "sX", deps=(cc_tok,))
            g2 = p.dma("sp", ctf[1][:, :], gview[128:256, :, :], "sX", deps=(cc_tok,))
            gather_tok = (g1, g2)

            outall = [lsb(f"oa{h}", [128, T], BF16) for h in range(H)]
            qbw = [lsb(f"qbw{i}", [128, 3 * NC_QR * 64], BF16) for i in range(2)]
            vbw = [lsb(f"vbw{i}", [128, 256], BF16) for i in range(2)]
            kbw = [lsb(f"kbw{i}", [128, 128], BF16) for i in range(2)]
            sH = ExitStack()
            lsh = lambda name, shape, dt: sH.enter_context(
                nc.sbuf_tensor(name + tg, shape, dt))
            qh = lsh("qh", [128, T], BF16)
            qpe = lsh("qpe", [64, T], BF16)
            qrt = lsh("qrt", [64, T], BF16)
            kh = lsh("kh", [128, 2 * T], BF16)
            vtm = lsh("vtm", [128, 2 * T], BF16)
            pt = [lsh(f"pt{i}", [128, 512], BF16) for i in range(2)]
            cmx = lsh("cmx", [128, 1], F32)
            cmx2 = lsh("cmx2", [128, 1], F32)
            bias_pp = lsh("bias_pp", [128, 1], F32)
            rca = lsh("rca", [1, 512], F32)
            rbc_t = lsh("rbc_t", [128, 512], F32)
            NKB = 2 * T // 128
            with ExitStack() as pl:
                psS = [pl.enter_context(nc.psum_tensor(f"psS{tg}{i}", [128, 512], F32))
                       for i in range(2)]
                psO = pl.enter_context(nc.psum_tensor(f"psO{tg}", [128, 512], F32))
                psSm = pl.enter_context(nc.psum_tensor(f"psSm{tg}", [1, 512], F32))
                psH = [pl.enter_context(nc.psum_tensor(f"psH{tg}{i}", [128, 512], F32))
                       for i in range(2)]
                for h in range(H):
                    hdone_prev = tok.get(f"{tg}sdone_{h-1}")
                    qw = qbw[h % 2]
                    qwl = wload(qw[:, :],
                                qb_b[h, :, :, :, :].rearrange("s k p i -> p s k i"),
                                3 * NC_QR * 64, deps=(tok.get(f"{tg}qhmm_{h-2}"),))
                    vw = vbw[h % 2]
                    vwl = wload(vw[:, :], kvb_v[h, :, :, :].rearrange("p k i -> p (k i)"),
                                256, deps=(tok.get(f"{tg}vmm_{h-2}"),))
                    kw = kbw[h % 2]
                    kwl = wload(kw[:, :], kvb_k[h, :, :, :].rearrange("p k i -> p (k i)"),
                                128, deps=(tok.get(f"{tg}knmm_{h-2}"),))
                    for tq in range(N_TQB):
                        b0 = psH[tq % 2]
                        b1 = psH[(tq + 1) % 2]
                        p.wait("pe", qwl, *qt_tok)
                        p.wait("pe", tok.get(f"{tg}qhe_{h}_{tq-1}"), hdone_prev)
                        p.guard(b0[0:64, :], qw[:, 0:64],
                                qtil[0][:, tq * 512:(tq + 1) * 512])
                        for k in range(NC_QR):
                            i = nc.tensor.matmul(
                                b0[0:64, :], qw[:, k * 64:(k + 1) * 64],
                                qtil[k][:, tq * 512:(tq + 1) * 512],
                                start=(k == 0), stop=(k == NC_QR - 1))
                        for k in range(NC_QR):
                            i = nc.tensor.matmul(
                                b1[0:64, :], qw[:, (NC_QR + k) * 64:(NC_QR + k + 1) * 64],
                                qtil[k][:, tq * 512:(tq + 1) * 512],
                                start=(k == 0), stop=(k == NC_QR - 1))
                        for k in range(NC_QR):
                            i = nc.tensor.matmul(
                                psO[0:64, :], qw[:, (2 * NC_QR + k) * 64:(2 * NC_QR + k + 1) * 64],
                                qtil[k][:, tq * 512:(tq + 1) * 512],
                                start=(k == 0), stop=(k == NC_QR - 1))
                        mt = p.inc(i, "sPE")
                        if tq == N_TQB - 1:
                            tok[f"{tg}qhmm_{h}"] = mt
                        p.wait("act", mt, tok.get(f"{tg}qrope_{h-1}"))
                        i = nc.scalar.copy(qh[0:64, tq * 512:(tq + 1) * 512], b0[0:64, :])
                        e1 = p.inc(i, "sACT")
                        p.wait("act", e1)
                        i = nc.scalar.copy(qpe[:, tq * 512:(tq + 1) * 512], b1[0:64, :])
                        e2 = p.inc(i, "sACT")
                        p.wait("act", e2)
                        i = nc.scalar.copy(qrt[:, tq * 512:(tq + 1) * 512], psO[0:64, :])
                        tok[f"{tg}qhe_{h}_{tq}"] = p.inc(i, "sACT")
                    qpe_tok = tok[f"{tg}qhe_{h}_{N_TQB-1}"]
                    p.wait("pool", qpe_tok, t1, t2, tok.get(f"{tg}qhdma_{h-1}"))
                    i = nc.gpsimd.tensor_mul(qpe[:, :], qpe[:, :], cosk[:, :])
                    ra = p.inc(i, "sPOOL")
                    i = nc.gpsimd.tensor_mul(qrt[:, :], qrt[:, :], sink[:, :])
                    rb = p.inc(i, "sPOOL")
                    p.wait("pool", ra, rb)
                    i = nc.gpsimd.tensor_add(qpe[:, :], qpe[:, :], qrt[:, :])
                    qrope0 = p.inc(i, "sPOOL")
                    qdma = p.dma("sp", qh[64:128, :], qpe[:, :], "sX", deps=(qrope0,))
                    qrope_tok = qdma
                    tok[f"{tg}qrope_{h}"] = qrope0
                    tok[f"{tg}qhdma_{h}"] = qdma
                    for nb in range(8):
                        bank = psH[nb % 2]
                        p.wait("pe", kwl, *gather_tok)
                        p.wait("pe", tok.get(f"{tg}kne_{h}_{nb-2}"), hdone_prev,
                               tok[f"{tg}qhe_{h}_{N_TQB-1}"])
                        p.guard(bank[0:64, :], kw[:, 0:64],
                                ctf[0][:, nb * 512:(nb + 1) * 512])
                        for kc in range(2):
                            i = nc.tensor.matmul(bank[0:64, :],
                                                 kw[:, kc * 64:(kc + 1) * 64],
                                                 ctf[kc][:, nb * 512:(nb + 1) * 512],
                                                 start=(kc == 0), stop=(kc == 1))
                        mt = p.inc(i, "sPE")
                        if nb == 7:
                            tok[f"{tg}knmm_{h}"] = mt
                        p.wait("act", mt)
                        i = nc.scalar.copy(kh[0:64, nb * 512:(nb + 1) * 512],
                                           bank[0:64, :])
                        tok[f"{tg}kne_{h}_{nb}"] = p.inc(i, "sACT")
                    kh0_tok = tok[f"{tg}kne_{h}_{7}"]
                    if h == 0:
                        tok[f"{tg}kpecp"] = p.dma("sp", kh[64:128, :],
                                                  gview[256:320, :, :], "sX",
                                                  deps=(cc_tok,))
                    kh_tok = (kh0_tok, tok[f"{tg}kpecp"])
                    for vb in range(8):
                        bank = psH[vb % 2]
                        p.wait("pe", vwl, *gather_tok)
                        p.wait("pe", tok.get(f"{tg}ve_{h}_{vb-2}"), hdone_prev,
                               tok[f"{tg}kne_{h}_{7}"])
                        p.guard(bank[:, 0:128], ctf[0][:, vb * 512:vb * 512 + 128],
                                vw[:, 0:128])
                        for j4 in range(4):
                            tkb = vb * 4 + j4
                            for kc in range(2):
                                i = nc.tensor.matmul(
                                    bank[:, j4 * 128:(j4 + 1) * 128],
                                    ctf[kc][:, tkb * 128:(tkb + 1) * 128],
                                    vw[:, kc * 128:(kc + 1) * 128],
                                    start=(kc == 0), stop=(kc == 1))
                        mt = p.inc(i, "sPE")
                        if vb == 7:
                            tok[f"{tg}vmm_{h}"] = mt
                        p.wait("act", mt)
                        i = nc.scalar.copy(vtm[:, vb * 512:(vb + 1) * 512], bank[:, :])
                        tok[f"{tg}ve_{h}_{vb}"] = p.inc(i, "sACT")
                    v_tok = tok[f"{tg}ve_{h}_{7}"]
                    for tq in range(N_TQB):
                        bank = psS[tq % 2]
                        p.wait("pe", *kh_tok)
                        p.wait("pe", qrope_tok, qpe_tok)
                        if tq >= 2:
                            p.wait("pe", (f"sDVE", 0) if False else tok[f"{tg}cm_{h}_{tq-2}"])
                        i = nc.tensor.matmul(bank[:, :], kh[:, 0:128],
                                             qh[:, tq * 512:(tq + 1) * 512],
                                             start=True, stop=True)
                        mt = p.inc(i, "sPE")
                        p.wait("dve", mt)
                        if tq == 0:
                            i = nc.vector.reduce_max(cmx[:, :], bank[:, :], axis=AX.X)
                        else:
                            i = nc.vector.reduce_max(cmx2[:, :], bank[:, :], axis=AX.X)
                        t = p.inc(i, "sDVE")
                        if tq > 0:
                            p.wait("dve", t)
                            i = nc.vector.tensor_max(cmx[:, :], cmx[:, :], cmx2[:, :])
                            t = p.inc(i, "sDVE")
                        tok[f"{tg}cm_{h}_{tq}"] = t
                    cm_tok = tok[f"{tg}cm_{h}_{N_TQB-1}"]
                    p.wait("pe", cm_tok)
                    i = nc.tensor.transpose(psSm[0:1, 0:128], cmx[:, 0:1], c_id[:, :])
                    tr_t = p.inc(i, "sPE")
                    p.wait("dve", tr_t)
                    i = nc.vector.reduce_max(rca[0:1, 0:1], psSm[0:1, 0:128], axis=AX.X)
                    t = p.inc(i, "sDVE")
                    p.wait("dve", t)
                    i = nc.vector.tensor_scalar_mul(rca[0:1, 1:2], rca[0:1, 0:1],
                                                    -SCALE_ATT)
                    t = p.inc(i, "sDVE")
                    p.wait("pe", t)
                    i = nc.tensor.matmul(psO[:, 0:1], c_orow[:, :], rca[0:1, 1:2],
                                         start=True, stop=True)
                    mt = p.inc(i, "sPE")
                    p.wait("act", mt)
                    i = nc.scalar.copy(bias_pp[:, :], psO[:, 0:1])
                    bias_tok = p.inc(i, "sACT")

                    for tqq in range(N_TQB):
                        for kb in range(NKB):
                            bank = psS[kb % 2]
                            p.wait("pe", *kh_tok)
                            p.wait("pe", qrope_tok, bias_tok,
                                   tok.get(f"{tg}exp_{h}_{tqq}_{kb-2}"))
                            if kb == 0:
                                p.wait("pe", tok.get(f"{tg}oev_{h}_{tqq-1}"),
                                       tok.get(f"{tg}oev_{h-1}_{N_TQB-1}"))
                            i = nc.tensor.matmul(bank[:, :], kh[:, kb * 128:(kb + 1) * 128],
                                                 qh[:, tqq * 512:(tqq + 1) * 512],
                                                 start=True, stop=True)
                            smt = p.inc(i, "sPE")
                            pb = pt[kb % 2]
                            p.wait("act", smt, bias_tok,
                                   tok.get(f"{tg}avmm_{h}_{tqq}_{kb-2}"))
                            i = nc.scalar.activation(pb[:, :], bank[:, :], AF.Exp,
                                                     bias=bias_pp[:, 0:1], scale=SCALE_ATT)
                            ept = p.inc(i, "sACT")
                            tok[f"{tg}exp_{h}_{tqq}_{kb}"] = ept
                            p.wait("pe", ept, v_tok)
                            if kb == 0:
                                p.guard(psO[:, :], vtm[:, 0:128], pb[:, :])
                            i = nc.tensor.matmul(psO[:, :], vtm[:, kb * 128:(kb + 1) * 128],
                                                 pb[:, :], start=(kb == 0),
                                                 stop=(kb == NKB - 1))
                            i = nc.tensor.matmul(psSm[0:1, :], c_ones[:, :], pb[:, :],
                                                 start=(kb == 0), stop=(kb == NKB - 1))
                            amt = p.inc(i, "sPE")
                            tok[f"{tg}avmm_{h}_{tqq}_{kb}"] = amt
                        p.wait("dve", tok[f"{tg}avmm_{h}_{tqq}_{NKB-1}"])
                        i = nc.vector.reciprocal(rca[0:1, :], psSm[0:1, :])
                        rt_ = p.inc(i, "sDVE")
                        p.wait("pe", rt_)
                        i = nc.tensor.matmul(psS[0][:, :], c_orow[:, :], rca[0:1, :],
                                             start=True, stop=True)
                        bmt = p.inc(i, "sPE")
                        p.wait("act", bmt)
                        i = nc.scalar.copy(rbc_t[:, :], psS[0][:, :])
                        rbt = p.inc(i, "sACT")
                        p.wait("dve", rbt)
                        i = nc.vector.tensor_mul(outall[h][:, tqq * 512:(tqq + 1) * 512],
                                                 psO[:, :], rbc_t[:, :])
                        tok[f"{tg}oev_{h}_{tqq}"] = p.inc(i, "sDVE")
                    tok[f"{tg}sdone_{h}"] = tok[f"{tg}oev_{h}_{N_TQB-1}"]

                p.barrier()
                sH.close()
                obw = [lsb(f"obw{i}", [128, H * 128], BF16) for i in range(2)]
                for r in range(NC_D):
                    ow = obw[r % 2]
                    owl = wload(ow[:, :], o_b[r, :, :, :].rearrange("h p d -> p h d"),
                                H * 128, deps=(tok.get(f"{tg}omm_{r-2}"),))
                    for tq in range(N_TQB):
                        un = r * N_TQB + tq
                        bank = psS[un % 2]
                        p.wait("pe", owl, tok[f"{tg}sdone_{H-1}"],
                               tok.get(f"{tg}oe_{un-2}"))
                        p.guard(bank[:, :], ow[:, 0:128],
                                outall[0][:, tq * 512:(tq + 1) * 512])
                        for hh2 in range(H):
                            i = nc.tensor.matmul(bank[:, :],
                                                 ow[:, hh2 * 128:(hh2 + 1) * 128],
                                                 outall[hh2][:, tq * 512:(tq + 1) * 512],
                                                 start=(hh2 == 0), stop=(hh2 == H - 1))
                        mt = p.inc(i, "sPE")
                        if tq == N_TQB - 1:
                            tok[f"{tg}omm_{r}"] = mt
                        xr = xres[un % 2]
                        rlt = p.dma("sp", xr[:, :], x1d[r, :, tq * 512:(tq + 1) * 512],
                                    "sX", deps=(tok.get(f"{tg}oe_{un-2}"),))
                        ob_ = oout[un % 2]
                        p.wait("dve", mt, rlt, tok.get(f"{tg}ost_{un-2}"))
                        i = nc.vector.tensor_add(ob_[:, :], bank[:, :], xr[:, :])
                        et = p.inc(i, "sDVE")
                        tok[f"{tg}oe_{un}"] = et
                        tok[f"{tg}ost_{un}"] = p.dma(
                            "sp", x2d[r, :, tq * 512:(tq + 1) * 512], ob_[:, :],
                            "sS", deps=(et,))

    emit_scale_prep()
    if PHASE_LIMIT is None or PHASE_LIMIT >= 2:
        emit_ffn(0, xT, xT, x1d if (PHASE_LIMIT is None or PHASE_LIMIT >= 3) else yout)
    if PHASE_LIMIT is None or PHASE_LIMIT >= 3:
        emit_attention()
    if PHASE_LIMIT is None:
        emit_ffn(1, x2d, x2d, yout)
    p.barrier()
    if DEBUG_Z:
        db = p.dma("sp", zdbg[:, :, :], z1d[:, :, :], "sS")
    if "sS" in p.cnt:
        p.wait("sp", ("sS", p.cnt["sS"]))
    stack.close()
    return nc


def _host_layout(inp):
    f32 = np.float32
    L = {}
    x = np.asarray(inp["L0_attn_x"], f32)
    shards = []
    for c in range(8):
        b, hh = c // 2, c % 2
        xs = np.ascontiguousarray(x[b, hh * T:(hh + 1) * T, :].T)
        shards.append(np.ascontiguousarray(xs.reshape(NC_D, 128, T)))
    W_up = np.asarray(inp["L0_ffn_W_up"], f32)
    W_gt = np.asarray(inp["L0_ffn_W_gate"], f32)
    W_dn = np.asarray(inp["L0_ffn_W_down"], f32)
    L["wup_b"] = np.ascontiguousarray(
        W_up.T.reshape(NC_D, 128, NC_I, 128).transpose(2, 1, 0, 3))
    L["wgt_b"] = np.ascontiguousarray(
        W_gt.T.reshape(NC_D, 128, NC_I, 128).transpose(2, 1, 0, 3))
    L["wdn_b"] = np.ascontiguousarray(
        W_dn.T.reshape(64, 64, 2, D).transpose(1, 0, 2, 3).reshape(NC_I, 128, D))
    qa = np.asarray(inp["L0_attn_q_a_w"], f32)
    qa_p = np.zeros((QRP, D), f32)
    qa_p[:QR] = qa
    L["qa_b"] = np.ascontiguousarray(
        qa_p.T.reshape(NC_D, 128, NC_QR, 128).transpose(2, 1, 0, 3))
    qb = np.asarray(inp["L0_attn_q_b_w"], f32)
    perm = np.arange(2048)
    for hh in range(H):
        base = hh * 128 + 64
        idx = np.arange(64).reshape(32, 2).T.reshape(64)
        perm[base:base + 64] = base + idx
    qb_perm = qb[perm]
    qb_p = np.zeros((QRP, D), f32)
    qb_p[:QR] = qb_perm.T
    rotp = np.concatenate([np.arange(32, 64), np.arange(0, 32)])
    qb_sets = np.zeros((H, 3, NC_QR, 128, 64), f32)
    for hh in range(H):
        cn = qb_p[:, hh * 128: hh * 128 + 64]
        cp = qb_p[:, hh * 128 + 64: hh * 128 + 128]
        cr = cp[:, rotp]
        for s3, cc in enumerate((cn, cp, cr)):
            qb_sets[hh, s3] = cc.reshape(NC_QR, 128, 64)
    L["qb_b"] = np.ascontiguousarray(qb_sets)
    kva = np.asarray(inp["L0_attn_kv_a_w"], f32)
    kperm = np.arange(320)
    idx = np.arange(64).reshape(32, 2).T.reshape(64)
    kperm[256:320] = 256 + idx
    kva_perm = kva[kperm]
    kva_p = np.zeros((KVAP, D), f32)
    kva_p[:320] = kva_perm
    L["kva_b"] = np.ascontiguousarray(
        kva_p.T.reshape(NC_D, 128, 3, 128).transpose(2, 1, 0, 3))
    rotp = np.concatenate([np.arange(32, 64), np.arange(0, 32)])
    kva_pe = kva_p.T[:, 256:320]
    L["kvar_b"] = np.ascontiguousarray(
        kva_pe[:, rotp].reshape(NC_D, 128, 64))
    kvb = np.asarray(inp["L0_attn_kv_b_w"], f32)
    kvbT = kvb.T
    vb = np.zeros((H, 128, 2, 128), f32)
    kb = np.zeros((H, 128, 2, 64), f32)
    for hh in range(H):
        vcols = kvbT[:, hh * 192 + 64: hh * 192 + 192]
        vb[hh] = vcols.reshape(2, 128, 128).transpose(1, 0, 2)
        kcols = kvbT[:, hh * 192: hh * 192 + 64]
        kb[hh] = kcols.reshape(2, 128, 64).transpose(1, 0, 2)
    L["kvb_v"] = np.ascontiguousarray(vb)
    L["kvb_k"] = np.ascontiguousarray(kb)
    ow = np.asarray(inp["L0_attn_o_w"], f32)
    L["o_b"] = np.ascontiguousarray(
        ow.T.reshape(H, 128, NC_D, 128).transpose(2, 0, 1, 3))
    L["su_ug"] = np.ascontiguousarray(np.asarray(inp["L0_ffn_SU_ug"], f32).reshape(NC_D, 128))
    L["su_dp"] = np.ascontiguousarray(np.asarray(inp["L0_ffn_SU_dp"], f32).reshape(NC_I, 128))
    wsug = np.asarray(inp["L0_ffn_Wscale_ug"], f32)
    L["ws_up"] = np.ascontiguousarray(wsug[:INTER].reshape(NC_I, 128))
    L["ws_gt"] = np.ascontiguousarray(wsug[INTER:].reshape(NC_I, 128))
    L["ws_dp"] = np.ascontiguousarray(np.asarray(inp["L0_ffn_Wscale_dp"], f32).reshape(NC_D, 128))
    qal = np.zeros((QRP,), f32)
    qal[:QR] = np.asarray(inp["L0_attn_q_a_ln"], f32)
    L["qaln"] = np.ascontiguousarray(qal.reshape(NC_QR, 128))
    L["kvln"] = np.ascontiguousarray(np.asarray(inp["L0_attn_kv_a_ln"], f32).reshape(2, 128))
    Hm = np.array([[1.0]], f32)
    while Hm.shape[0] < 128:
        Hm = np.block([[Hm, Hm], [Hm, -Hm]])
    L["h128c"] = Hm.astype(ml_dtypes.bfloat16)
    Hm64 = np.array([[1.0]], f32)
    while Hm64.shape[0] < 64:
        Hm64 = np.block([[Hm64, Hm64], [Hm64, -Hm64]])
    L["hd64c"] = np.kron(Hm64, np.eye(2, dtype=f32)).astype(ml_dtypes.bfloat16)
    L["onescol"] = np.ones((128, 1), ml_dtypes.bfloat16)
    L["onesrow"] = np.ones((1, 128), f32)
    L["identc"] = np.eye(128, dtype=f32)
    inv = 1.0 / THETA ** (np.arange(0, ROPE, 2, dtype=np.float64) / ROPE)
    sgn = np.concatenate([-np.ones(32), np.ones(32)])
    cos_t, sin_t = [], []
    for c in range(8):
        hh = c % 2
        pos = np.arange(hh * T, (hh + 1) * T, dtype=np.float64)
        fr = pos[None, :] * inv[:, None]
        cc = np.concatenate([np.cos(fr), np.cos(fr)], 0)
        ss = np.concatenate([np.sin(fr), np.sin(fr)], 0) * sgn[:, None]
        cos_t.append(np.ascontiguousarray(cc.astype(ml_dtypes.bfloat16)))
        sin_t.append(np.ascontiguousarray(ss.astype(ml_dtypes.bfloat16)))
    return L, shards, cos_t, sin_t


def kernel(**inputs):
    L, shards, cos_t, sin_t = _host_layout(inputs)
    nc = build_program()
    in_maps = []
    for c in range(8):
        m = dict(L)
        m["xT"] = shards[c]
        m["cosq"] = cos_t[c]
        m["sinq"] = sin_t[c]
        in_maps.append(m)
    res = run_bass_kernel_spmd(nc, in_maps, core_ids=list(range(8)))
    out = np.empty((B, S, D), np.float32)
    for c in range(8):
        b, hh = c // 2, c % 2
        yc = res.results[c]["yout"].reshape(D, T)
        out[b, hh * T:(hh + 1) * T, :] = yc.T
    return out


# version 06
# speedup vs baseline: 1.0514x; 1.0514x over previous
import numpy as np
import ml_dtypes
from contextlib import ExitStack

import concourse.bass as bass
import concourse.mybir as mybir
from concourse.bass_utils import run_bass_kernel_spmd

F32 = mybir.dt.float32
BF16 = mybir.dt.bfloat16
AF = mybir.ActivationFunctionType
ALU = mybir.AluOpType
AX = mybir.AxisListType

B, S, D = 4, 4096, 2048
T = 2048
NC_D = 16
INTER = 8192
NC_I = 64
H, QK, VD, QR, KVR, ROPE = 16, 128, 128, 682, 256, 64
QRP = 768
NC_QR = 6
KVAP = 384
THETA = 128000.0
SCALE_ATT = float(1.0 / np.sqrt(128.0))
N_TQB = 4
PHASE_LIMIT = None
DEBUG_Z = False
DEBUG_W = False


class Prog:
    def __init__(self, nc, stack):
        self.nc = nc
        self.stack = stack
        self.eng = {"pe": nc.tensor, "act": nc.scalar, "dve": nc.vector,
                    "pool": nc.gpsimd, "sp": nc.sync}
        self.sems = {}
        self.cnt = {}
        self.waited = {}

    def sem(self, name):
        if name not in self.sems:
            self.sems[name] = self.stack.enter_context(self.nc.semaphore(name))
            self.cnt[name] = 0
        return self.sems[name]

    def inc(self, inst, sname, n=1):
        h = self.sem(sname)
        self.cnt[sname] += n
        inst.then_inc(h, n)
        return (sname, self.cnt[sname])

    def wait(self, ename, *tokens):
        e = self.eng[ename]
        for t in tokens:
            if t is None:
                continue
            s, v = t
            key = (ename, s)
            if v > self.waited.get(key, 0):
                e.wait_ge(self.sems[s], v)
                self.waited[key] = v
                if ename == "pe":
                    self.pe_stall = True

    def guard(self, bank, lhsT, rhs):
        if getattr(self, "pe_stall", False):
            for _ in range(4):
                self.nc.tensor.matmul(bank, lhsT, rhs, start=True, stop=True)
            self.pe_stall = False

    def barrier(self):
        toks = [(s, c) for s, c in self.cnt.items() if c > 0]
        for e in ("pe", "act", "dve", "pool", "sp"):
            self.wait(e, *toks)

    def dma(self, qname, out, in_, sname, deps=()):
        self.wait(qname, *deps)
        inst = self.eng[qname].dma_start(out, in_)
        return self.inc(inst, sname, 16)


def build_program():
    nc = bass.Bass("TRN2", target_bir_lowering=False)
    stack = ExitStack()
    p = Prog(nc, stack)

    def dram(name, shape, dt, kind="ExternalInput"):
        return nc.dram_tensor(name, shape, dt, kind=kind).ap()

    xT = dram("xT", [NC_D, 128, T], F32)
    wup_b = dram("wup_b", [NC_I, 128, NC_D, 128], F32)
    wgt_b = dram("wgt_b", [NC_I, 128, NC_D, 128], F32)
    wdn_b = dram("wdn_b", [NC_I, 128, D], F32)
    qa_b = dram("qa_b", [NC_QR, 128, NC_D, 128], F32)
    qb_b = dram("qb_b", [H, 3, NC_QR, 128, 64], F32)
    kva_b = dram("kva_b", [3, 128, NC_D, 128], F32)
    kvar_b = dram("kvar_b", [NC_D, 128, 64], F32)
    kvb_v = dram("kvb_v", [H, 128, 2, 128], F32)
    kvb_k = dram("kvb_k", [H, 128, 2, 64], F32)
    o_b = dram("o_b", [NC_D, H, 128, 128], F32)
    su_ug = dram("su_ug", [NC_D, 128], F32)
    su_dp = dram("su_dp", [NC_I, 128], F32)
    ws_up = dram("ws_up", [NC_I, 128], F32)
    ws_gt = dram("ws_gt", [NC_I, 128], F32)
    ws_dp = dram("ws_dp", [NC_D, 128], F32)
    qaln = dram("qaln", [NC_QR, 128], F32)
    kvln = dram("kvln", [2, 128], F32)
    h128c = dram("h128c", [128, 128], BF16)
    hd64c = dram("hd64c", [128, 128], BF16)
    onescol = dram("onescol", [128, 1], BF16)
    onesrow = dram("onesrow", [1, 128], F32)
    identc = dram("identc", [128, 128], F32)
    cosq = dram("cosq", [64, T], BF16)
    sinq = dram("sinq", [64, T], BF16)

    yout = dram("yout", [NC_D, 128, T], F32, kind="ExternalOutput")

    z1d = dram("z1d", [NC_I, 128, T], BF16, kind="Internal")
    x1d = dram("x1d", [NC_D, 128, T], F32, kind="Internal")
    x2d = dram("x2d", [NC_D, 128, T], F32, kind="Internal")
    ccs = dram("ccs", [KVR + ROPE, T], BF16, kind="Internal")
    ccd = dram("ccd", [2 * (KVR + ROPE), T], BF16, kind="Internal")
    if DEBUG_Z:
        zdbg = dram("zdbg", [NC_I, 128, T], BF16, kind="ExternalOutput")
    if DEBUG_W:
        wdbg = dram("wdbg", [128, NC_I * 1024], BF16, kind="ExternalOutput")

    sb = lambda name, shape, dt: stack.enter_context(nc.sbuf_tensor(name, shape, dt))

    c_h128 = sb("c_h128", [128, 128], BF16)
    c_hd64 = sb("c_hd64", [128, 128], BF16)
    c_ones = sb("c_ones", [128, 1], BF16)
    c_orow = sb("c_orow", [1, 128], F32)
    c_id = sb("c_id", [128, 128], F32)
    pp = sb("pp", [128, 256], F32)
    PP_SU, PP_SUP, PP_SGT, PP_WDP, PP_QALN, PP_KVLN = 0, 16, 80, 144, 160, 166

    rowA = sb("rowA", [1, T], F32)
    rowB = sb("rowB", [1, T], F32)

    stg = [sb(f"stg{i}", [128, 2048], F32) for i in range(2)]
    xres = [sb(f"xres{i}", [128, 512], F32) for i in range(2)]
    oout = [sb(f"oout{i}", [128, 512], F32) for i in range(2)]

    tok = {}
    last_store = [None]
    stg_free = [(), ()]
    stg_idx = [0]

    def stg_next():
        si = stg_idx[0] % 2
        stg_idx[0] += 1
        return si

    def wload(dst_ap, src_ap, width, deps=()):
        si = stg_next()
        t1 = p.dma("sp", stg[si][:, 0:width], src_ap, "sW", deps=stg_free[si])
        p.wait("pool", t1, *[d for d in deps if d is not None])
        i = nc.gpsimd.tensor_copy(dst_ap, stg[si][:, 0:width])
        tk = p.inc(i, "sPOOL")
        stg_free[si] = (tk,)
        return tk

    def emit_scale_prep():
        with ExitStack() as lst:
            pst = lst.enter_context(nc.psum_tensor("pst", [128, 128], F32))
            tmp = lst.enter_context(nc.sbuf_tensor("tmp_sc", [128, 128], F32))
            tmp2 = lst.enter_context(nc.sbuf_tensor("tmp_sc2", [128, 128], F32))
            p.dma("sp", c_h128[:, :], h128c[:, :], "sX")
            p.dma("sp", c_hd64[:, :], hd64c[:, :], "sX")
            p.dma("sp", c_ones[:, :], onescol[:, :], "sX")
            p.dma("sp", c_orow[:, :], onesrow[:, :], "sX")
            t5 = p.dma("sp", c_id[:, :], identc[:, :], "sX")
            tok["consts"] = t5

            def tr_to_pp(src_ap, n, col0, scale, pre=None):
                tload = p.dma("sp", tmp[0:n, :], src_ap, "sX",
                              deps=(tok.get("pst_free"),))
                ttok = tload
                if pre is not None:
                    tload2 = p.dma("sp", tmp2[0:n, :], pre, "sX")
                    p.wait("dve", tload, tload2)
                    i = nc.vector.tensor_mul(tmp[0:n, :], tmp[0:n, :], tmp2[0:n, :])
                    ttok = p.inc(i, "sDVE")
                p.wait("pe", ttok, t5, tok.get("pst_free"))
                i = nc.tensor.transpose(pst[:, 0:n], tmp[0:n, :], c_id[0:n, 0:n])
                mtok = p.inc(i, "sPE")
                p.wait("act", mtok)
                i = nc.scalar.activation(pp[:, col0:col0 + n], pst[:, 0:n], AF.Identity,
                                         bias=0.0, scale=float(scale))
                etok = p.inc(i, "sACT")
                tok["pst_free"] = etok
                return etok

            tr_to_pp(su_ug[:, :], NC_D, PP_SU, 1.0 / (64.0 * np.sqrt(2048.0)))
            tr_to_pp(ws_up[:, :], NC_I, PP_SUP, 1.0 / np.sqrt(8192.0), pre=su_dp[:, :])
            tr_to_pp(ws_gt[:, :], NC_I, PP_SGT, 64.0)
            tr_to_pp(ws_dp[:, :], NC_D, PP_WDP, 32.0)
            tr_to_pp(qaln[:, :], NC_QR, PP_QALN, 1.0)
            tok["pp_done"] = tr_to_pp(kvln[:, :], 2, PP_KVLN, 1.0)

    def emit_ln_stats(src_chunks, eps, tag, mu_bc, al_bc, xb16, sqb):
        with ExitStack() as lst:
            psx = lst.enter_context(nc.psum_tensor("psx" + tag, [1, T], F32))
            psx2 = lst.enter_context(nc.psum_tensor("psx2" + tag, [1, T], F32))
            for a in range(NC_D):
                si = stg_next()
                xf = stg[si]
                xb = xb16[a % 2]
                sq = sqb[a % 2]
                lt = p.dma("sp", xf[:, :], src_chunks(a), "sX", deps=stg_free[si])
                p.wait("pool", lt, tok.get(f"{tag}_mmdone_{a-2}"))
                i = nc.gpsimd.tensor_copy(xb[:, :], xf[:, :])
                ct = p.inc(i, "sPOOL")
                tok[f"{tag}_cast_{a}"] = ct
                p.wait("act", lt, tok.get(f"{tag}_sqmm_{a-2}"))
                i = nc.scalar.activation(sq[:, :], xf[:, :], AF.Square)
                sqt = p.inc(i, "sACT")
                tok[f"{tag}_sq_{a}"] = sqt
                stg_free[si] = (ct, sqt)
                p.wait("pe", ct, sqt)
                if a == 0:
                    p.guard(psx[0:1, 0:512], c_ones[:, :], xb[:, 0:512])
                for tq in range(N_TQB):
                    i = nc.tensor.matmul(psx[0:1, tq * 512:(tq + 1) * 512], c_ones[:, :],
                                         xb[:, tq * 512:(tq + 1) * 512],
                                         start=(a == 0), stop=(a == NC_D - 1))
                tok[f"{tag}_mmdone_{a}"] = p.inc(i, "sPE")
                for tq in range(N_TQB):
                    i = nc.tensor.matmul(psx2[0:1, tq * 512:(tq + 1) * 512], c_ones[:, :],
                                         sq[:, tq * 512:(tq + 1) * 512],
                                         start=(a == 0), stop=(a == NC_D - 1))
                tok[f"{tag}_sqmm_{a}"] = p.inc(i, "sPE")
            mmx = tok[f"{tag}_mmdone_{NC_D-1}"]
            mmx2 = tok[f"{tag}_sqmm_{NC_D-1}"]
            rowS = mu_bc[0:1, :]
            p.wait("dve", mmx, mmx2)
            i = nc.vector.tensor_scalar_mul(rowA[:, :], psx[0:1, :], 1.0 / 2048.0)
            t = p.inc(i, "sDVE")
            p.wait("dve", t)
            i = nc.vector.tensor_scalar_mul(rowS, psx2[0:1, :], 1.0 / 2048.0)
            t = p.inc(i, "sDVE")
            p.wait("dve", t)
            i = nc.vector.tensor_mul(rowB[:, :], rowA[:, :], rowA[:, :])
            t = p.inc(i, "sDVE")
            p.wait("dve", t)
            i = nc.vector.tensor_sub(rowS, rowS, rowB[:, :])
            trow = p.inc(i, "sDVE")
            p.wait("dve", trow)
            i = nc.vector.tensor_scalar_add(rowS, rowS, float(eps))
            trow = p.inc(i, "sDVE")
            p.wait("act", trow)
            i = nc.scalar.activation(rowB[:, :], rowS, AF.Ln)
            t = p.inc(i, "sACT")
            p.wait("act", t)
            i = nc.scalar.activation(rowB[:, :], rowB[:, :], AF.Exp, scale=-0.5)
            ta = p.inc(i, "sACT")
        with ExitStack() as lst:
            bcp = lst.enter_context(nc.psum_tensor("bcps" + tag, [128, 512], F32))
            last = None
            for (row, dst) in ((rowA, mu_bc), (rowB, al_bc)):
                for tq in range(N_TQB):
                    p.wait("pe", ta, tok.get(f"{tag}bcp_free"))
                    i = nc.tensor.matmul(bcp[:, :], c_orow[:, :],
                                         row[0:1, tq * 512:(tq + 1) * 512],
                                         start=True, stop=True)
                    mt = p.inc(i, "sPE")
                    p.wait("act", mt)
                    i = nc.scalar.copy(dst[:, tq * 512:(tq + 1) * 512], bcp[:, :])
                    et = p.inc(i, "sACT")
                    tok[f"{tag}bcp_free"] = et
                    last = et
        return (last,)

    def emit_ffn(passidx, src_dram, res_dram, out_dram):
        tg = f"f{passidx}"
        p.barrier()
        with ExitStack() as lst:
            lsb = lambda name, shape, dt: lst.enter_context(
                nc.sbuf_tensor(name + tg, shape, dt))
            mu_bc = lsb("mu_bc", [128, T], F32)
            al_bc = lsb("al_bc", [128, T], F32)
            xb16 = [lsb(f"xb{i}", [128, T], BF16) for i in range(2)]
            sqb = [lsb(f"sq{i}", [128, T], BF16) for i in range(2)]
            ln_tok = emit_ln_stats(lambda a: src_dram[a, :, :], 1e-5, tg + "ln",
                                   mu_bc, al_bc, xb16, sqb)
            v = [lsb(f"v{a}", [128, T], BF16) for a in range(NC_D)]
            for a in range(NC_D):
                si = stg_next()
                buf = stg[si]
                lt = p.dma("sp", buf[:, :], src_dram[a, :, :], "sX",
                           deps=stg_free[si])
                p.wait("dve", lt, *ln_tok)
                p.wait("dve", tok.get(f"{tg}v_done_{a-2}"))
                i = nc.vector.tensor_sub(v[a][:, :], buf[:, :], mu_bc[:, :])
                t = p.inc(i, "sDVE")
                stg_free[si] = (t,)
                p.wait("dve", t)
                i = nc.vector.scalar_tensor_tensor(
                    v[a][:, :], v[a][:, :], pp[:, PP_SU + a:PP_SU + a + 1], al_bc[:, :],
                    ALU.mult, ALU.mult)
                tok[f"{tg}v_done_{a}"] = p.inc(i, "sDVE")
            bf_prev = [tok[f"{tg}v_done_{a}"] for a in range(NC_D)]
            for st_i in range(4):
                bit = 1 << st_i
                pairs = [(a, a | bit) for a in range(NC_D) if not (a & bit)]
                res_t = {}
                for k, (a, b_) in enumerate(pairs):
                    en = "pool" if k % 2 == 0 else "dve"
                    sn = "sPOOL" if k % 2 == 0 else "sDVE"
                    p.wait(en, bf_prev[a], bf_prev[b_])
                    i = p.eng[en].tensor_add(v[a][:, :], v[a][:, :], v[b_][:, :])
                    ta_ = p.inc(i, sn)
                    p.wait("dve", ta_)
                    i = nc.vector.scalar_tensor_tensor(v[b_][:, :], v[b_][:, :], -2.0,
                                                       v[a][:, :], ALU.mult, ALU.add)
                    tb_ = p.inc(i, "sDVE")
                    res_t[a] = tb_
                    res_t[b_] = tb_
                for a2, t2 in res_t.items():
                    bf_prev[a2] = t2
            with ExitStack() as pl:
                psu = [pl.enter_context(nc.psum_tensor(f"psu{tg}{i}", [128, 512], F32))
                       for i in range(2)]
                for a in range(NC_D):
                    for tq in range(N_TQB):
                        ui = a * N_TQB + tq
                        bank = psu[ui % 2]
                        p.wait("pe", bf_prev[a], tok.get(f"{tg}ue_{ui-2}"),
                               tok.get("consts"))
                        i = nc.tensor.matmul(bank[:, :], c_h128[:, :],
                                             v[a][:, tq * 512:(tq + 1) * 512],
                                             start=True, stop=True)
                        mt = p.inc(i, "sPE")
                        p.wait("act", mt)
                        i = nc.scalar.copy(v[a][:, tq * 512:(tq + 1) * 512], bank[:, :])
                        tok[f"{tg}ue_{ui}"] = p.inc(i, "sACT")
            u_last = tok[f"{tg}ue_{NC_D * N_TQB - 1}"]

            wub = [lsb(f"wub{i}", [128, NC_D * 128], BF16) for i in range(2)]
            wgb = [lsb(f"wgb{i}", [128, NC_D * 128], BF16) for i in range(2)]
            gbuf = [lsb(f"gb{i}", [128, 512], BF16) for i in range(2)]
            hbuf = [lsb(f"hb{i}", [128, 512], BF16) for i in range(2)]
            z1sb = [lsb(f"z1sb{i}", [128, T], BF16) for i in range(2)]
            with ExitStack() as pl:
                psU = [pl.enter_context(nc.psum_tensor(f"psU{tg}{i}", [128, 512], F32))
                       for i in range(2)]
                psG = [pl.enter_context(nc.psum_tensor(f"psG{tg}{i}", [128, 512], F32))
                       for i in range(2)]
                psZ = [pl.enter_context(nc.psum_tensor(f"psZ{tg}{i}", [128, 512], F32))
                       for i in range(2)]
                for c in range(NC_I):
                    wu = wub[c % 2]
                    wg = wgb[c % 2]
                    wlt = wload(wu[:, :], wup_b[c, :, :, :].rearrange("p k i -> p (k i)"),
                                NC_D * 128, deps=(tok.get(f"{tg}upmm_{c-2}"),))
                    wlt2 = wload(wg[:, :], wgt_b[c, :, :, :].rearrange("p k i -> p (k i)"),
                                 NC_D * 128, deps=(tok.get(f"{tg}gtmm_{c-2}"),))
                    zs = z1sb[c % 2]
                    for tq in range(N_TQB):
                        un = c * N_TQB + tq
                        bu, bg, bz = psU[un % 2], psG[un % 2], psZ[un % 2]
                        p.wait("pe", wlt, u_last, tok.get(f"{tg}hh_{un-2}"))
                        p.guard(bu[:, :], wu[:, 0:128], v[0][:, tq * 512:(tq + 1) * 512])
                        for k in range(NC_D):
                            i = nc.tensor.matmul(bu[:, :], wu[:, k * 128:(k + 1) * 128],
                                                 v[k][:, tq * 512:(tq + 1) * 512],
                                                 start=(k == 0), stop=(k == NC_D - 1))
                        tok_up = p.inc(i, "sPE")
                        if tq == N_TQB - 1:
                            tok[f"{tg}upmm_{c}"] = tok_up
                        p.wait("pe", wlt2, tok.get(f"{tg}sil_{un-2}"))
                        p.guard(bg[:, :], wg[:, 0:128], v[0][:, tq * 512:(tq + 1) * 512])
                        for k in range(NC_D):
                            i = nc.tensor.matmul(bg[:, :], wg[:, k * 128:(k + 1) * 128],
                                                 v[k][:, tq * 512:(tq + 1) * 512],
                                                 start=(k == 0), stop=(k == NC_D - 1))
                        tok_gt = p.inc(i, "sPE")
                        if tq == N_TQB - 1:
                            tok[f"{tg}gtmm_{c}"] = tok_gt
                        g = gbuf[un % 2]
                        hh = hbuf[un % 2]
                        p.wait("act", tok_gt, tok.get(f"{tg}hh_{un-2}"))
                        i = nc.scalar.activation(g[:, :], bg[:, :], AF.Silu, bias=0.0,
                                                 scale=pp[:, PP_SGT + c:PP_SGT + c + 1])
                        sil = p.inc(i, "sACT")
                        tok[f"{tg}sil_{un}"] = sil
                        p.wait("dve", sil, tok_up, tok.get(f"{tg}zmm_{un-2}"))
                        i = nc.vector.scalar_tensor_tensor(
                            hh[:, :], bu[:, :], pp[:, PP_SUP + c:PP_SUP + c + 1], g[:, :],
                            ALU.mult, ALU.mult)
                        hht = p.inc(i, "sDVE")
                        tok[f"{tg}hh_{un}"] = hht
                        p.wait("pe", hht, tok.get(f"{tg}ze_{un-2}"))
                        i = nc.tensor.matmul(bz[:, :], c_h128[:, :], hh[:, :],
                                             start=True, stop=True)
                        zmt = p.inc(i, "sPE")
                        tok[f"{tg}zmm_{un}"] = zmt
                        p.wait("act", zmt, tok.get(f"{tg}zw_{c-2}"))
                        i = nc.scalar.copy(zs[:, tq * 512:(tq + 1) * 512], bz[:, :])
                        tok[f"{tg}ze_{un}"] = p.inc(i, "sACT")
                    tok[f"{tg}zw_{c}"] = p.dma(
                        "sp", z1d[c, :, :], zs[:, :], "sS",
                        deps=(tok[f"{tg}ze_{c * N_TQB + N_TQB - 1}"],))
            spill_done = tok[f"{tg}zw_{NC_I - 1}"]

        p.barrier()
        with ExitStack() as lst:
            lsb = lambda name, shape, dt: lst.enter_context(
                nc.sbuf_tensor(name + tg, shape, dt))
            wstrip = lsb("wstrip", [128, NC_I * 1024], BF16)
            wdl = [lsb(f"wdl{i}", [128, 1024], BF16) for i in range(4)]
            ztile = [lsb(f"zt{i}", [128, 512], BF16) for i in range(4)]
            for rh in range(2):
                with ExitStack() as pl:
                    psW = [pl.enter_context(
                        nc.psum_tensor(f"psW{tg}{rh}{i}", [128, 512], F32))
                        for i in range(2)]
                    prev_mm = tok.get(f"{tg}dmm_{rh-1}_{N_TQB-1}_{NC_I-1}")
                    p.wait("pe", tok.get(f"{tg}de_{rh-1}_{N_TQB-1}"))
                    for m in range(NC_I):
                        wb = wdl[m % 4]
                        wlt = wload(wb[:, :], wdn_b[m, :, rh * 1024:(rh + 1) * 1024],
                                    1024, deps=(tok.get(f"{tg}wse_{rh}_{m-4}"),))
                        for j in range(2):
                            bank = psW[(2 * m + j) % 2]
                            p.wait("pe", wlt, tok.get(f"{tg}wse2_{rh}_{2*m+j-2}"))
                            i = nc.tensor.matmul(bank[:, :], c_hd64[:, :],
                                                 wb[:, j * 512:(j + 1) * 512],
                                                 start=True, stop=True)
                            mt = p.inc(i, "sPE")
                            p.wait("act", mt, prev_mm)
                            i = nc.scalar.copy(
                                wstrip[:, m * 1024 + j * 512:m * 1024 + (j + 1) * 512],
                                bank[:, :])
                            et = p.inc(i, "sACT")
                            tok[f"{tg}wse2_{rh}_{2*m+j}"] = et
                            if j == 1:
                                tok[f"{tg}wse_{rh}_{m}"] = et
                    wdone = tok[f"{tg}wse_{rh}_{NC_I - 1}"]
                    if DEBUG_W and rh == 0 and passidx == 0:
                        p.dma("sp", wdbg[:, :], wstrip[:, :], "sS", deps=(wdone,))
                with ExitStack() as pl:
                    psD = [pl.enter_context(
                        nc.psum_tensor(f"psD{tg}{rh}_{r}", [128, 512], F32))
                        for r in range(8)]
                    for tq in range(N_TQB):
                        for m in range(NC_I):
                            zt = ztile[m % 4]
                            zlt = p.dma("sp", zt[:, :],
                                        z1d[:, 2 * m:2 * m + 2, tq * 512:(tq + 1) * 512],
                                        "sX",
                                        deps=(spill_done,
                                              tok.get(f"{tg}dmm_{rh}_{tq}_{m-4}"),))
                            p.wait("pe", zlt, wdone)
                            if m == 0:
                                p.wait("pe", tok.get(f"{tg}de_{rh}_{tq-1}"))
                                p.guard(psD[0][:, :],
                                        wstrip[:, m * 1024:m * 1024 + 128], zt[:, :])
                            for r8 in range(8):
                                i = nc.tensor.matmul(
                                    psD[r8][:, :],
                                    wstrip[:, m * 1024 + r8 * 128:m * 1024 + (r8 + 1) * 128],
                                    zt[:, :], start=(m == 0), stop=(m == NC_I - 1))
                            tok[f"{tg}dmm_{rh}_{tq}_{m}"] = p.inc(i, "sPE")
                        dmt = tok[f"{tg}dmm_{rh}_{tq}_{NC_I - 1}"]
                        ev_last = None
                        for r8 in range(8):
                            r = rh * 8 + r8
                            xr = xres[r8 % 2]
                            rlt = p.dma("sp", xr[:, :],
                                        res_dram[r, :, tq * 512:(tq + 1) * 512], "sX",
                                        deps=(tok.get(f"{tg}dev_{rh}_{tq}_{r8-2}"),))
                            ob = oout[r8 % 2]
                            p.wait("dve", dmt, rlt, tok.get(f"{tg}dst_{rh}_{tq}_{r8-2}"))
                            i = nc.vector.scalar_tensor_tensor(
                                ob[:, :], psD[r8][:, :], pp[:, PP_WDP + r:PP_WDP + r + 1],
                                xr[:, :], ALU.mult, ALU.add)
                            et = p.inc(i, "sDVE")
                            tok[f"{tg}dev_{rh}_{tq}_{r8}"] = et
                            tok[f"{tg}dst_{rh}_{tq}_{r8}"] = p.dma(
                                "sp", out_dram[r, :, tq * 512:(tq + 1) * 512],
                                ob[:, :], "sS", deps=(et,))
                            ev_last = et
                        tok[f"{tg}de_{rh}_{tq}"] = ev_last
            last_store[0] = tok[f"{tg}dst_1_{N_TQB-1}_7"]

    def emit_attention():
        tg = "at"
        p.barrier()
        with ExitStack() as lst:
            lsb = lambda name, shape, dt: lst.enter_context(
                nc.sbuf_tensor(name + tg, shape, dt))
            qtil = [lsb(f"qtil{j}", [128, T], BF16) for j in range(NC_QR)]
            cosk = lsb("cosk", [64, T], BF16)
            sink = lsb("sink", [64, T], BF16)
            t1 = p.dma("sp", cosk[:, :], cosq[:, :], "sX")
            t2 = p.dma("sp", sink[:, :], sinq[:, :], "sX")
            sA = ExitStack()
            lsa = lambda name, shape, dt: sA.enter_context(
                nc.sbuf_tensor(name + tg, shape, dt))
            mu_bc = lsa("mu_bc", [128, T], F32)
            al_bc = lsa("al_bc", [128, T], F32)
            xb16 = [lsa(f"xb{i}", [128, T], BF16) for i in range(2)]
            sqb = [lsa(f"sq{i}", [128, T], BF16) for i in range(2)]
            ln_tok = emit_ln_stats(lambda a: x1d[a, :, :], 1e-5, tg + "ln",
                                   mu_bc, al_bc, xb16, sqb)
            y = [lsa(f"y{a}", [128, T], BF16) for a in range(NC_D)]
            for a in range(NC_D):
                si = stg_next()
                buf = stg[si]
                lt = p.dma("sp", buf[:, :], x1d[a, :, :], "sX",
                           deps=stg_free[si])
                p.wait("dve", lt, *ln_tok)
                p.wait("dve", tok.get(f"{tg}y_{a-2}"))
                i = nc.vector.tensor_sub(y[a][:, :], buf[:, :], mu_bc[:, :])
                t = p.inc(i, "sDVE")
                stg_free[si] = (t,)
                p.wait("dve", t)
                i = nc.vector.tensor_mul(y[a][:, :], y[a][:, :], al_bc[:, :])
                tok[f"{tg}y_{a}"] = p.inc(i, "sDVE")
            y_last = tok[f"{tg}y_{NC_D-1}"]

            kvw = [lsa(f"kvw{i}", [128, NC_D * 128], BF16) for i in range(2)]
            cf32 = [lsa(f"cf{j}", [128, T], BF16) for j in range(2)]
            kpe = lsa("kpe", [64, T], BF16)
            csq = lsa("csq", [128, T], BF16)
            ctil = cf32
            kro = lsa("kro", [64, T], BF16)
            ksc = lsa("ksc", [64, T], BF16)
            with ExitStack() as pl:
                psA = [pl.enter_context(nc.psum_tensor(f"psA{tg}{i}", [128, 512], F32))
                       for i in range(2)]
                psR = pl.enter_context(nc.psum_tensor(f"psR{tg}", [1, T], F32))
                bcp = pl.enter_context(nc.psum_tensor(f"bcpA{tg}", [128, 512], F32))
                for ic in range(3):
                    wb = kvw[ic % 2]
                    wlt = wload(wb[:, :], kva_b[ic, :, :, :].rearrange("p k i -> p (k i)"),
                                NC_D * 128, deps=(tok.get(f"{tg}kvmm_{ic-2}"),))
                    for tq in range(N_TQB):
                        un = ic * N_TQB + tq
                        bank = psA[un % 2]
                        p.wait("pe", wlt, y_last, tok.get(f"{tg}kve_{un-2}"))
                        p.guard(bank[:, :], wb[:, 0:128], y[0][:, tq * 512:(tq + 1) * 512])
                        for k in range(NC_D):
                            i = nc.tensor.matmul(bank[:, :], wb[:, k * 128:(k + 1) * 128],
                                                 y[k][:, tq * 512:(tq + 1) * 512],
                                                 start=(k == 0), stop=(k == NC_D - 1))
                        mt = p.inc(i, "sPE")
                        if tq == N_TQB - 1:
                            tok[f"{tg}kvmm_{ic}"] = mt
                        p.wait("act", mt)
                        if ic < 2:
                            i = nc.scalar.copy(cf32[ic][:, tq * 512:(tq + 1) * 512],
                                               bank[:, :])
                        else:
                            i = nc.scalar.copy(kpe[:, tq * 512:(tq + 1) * 512],
                                               bank[0:64, :])
                        tok[f"{tg}kve_{un}"] = p.inc(i, "sACT")
                for j in range(2):
                    p.wait("act", tok[f"{tg}kve_{j * N_TQB + N_TQB - 1}"])
                    i = nc.scalar.activation(csq[:, :], cf32[j][:, :], AF.Square)
                    sqt = p.inc(i, "sACT")
                    p.wait("pe", sqt)
                    for tq in range(N_TQB):
                        i = nc.tensor.matmul(psR[0:1, tq * 512:(tq + 1) * 512],
                                             c_ones[:, :],
                                             csq[:, tq * 512:(tq + 1) * 512],
                                             start=(j == 0), stop=(j == 1))
                    csqmm = p.inc(i, "sPE")
                p.wait("dve", csqmm)
                rowS = al_bc[0:1, :]
                i = nc.vector.tensor_scalar_mul(rowS, psR[0:1, :], 1.0 / 256.0)
                t = p.inc(i, "sDVE")
                p.wait("dve", t)
                i = nc.vector.tensor_scalar_add(rowS, rowS, 1e-6)
                t = p.inc(i, "sDVE")
                p.wait("act", t)
                i = nc.scalar.activation(rowB[:, :], rowS, AF.Ln)
                t = p.inc(i, "sACT")
                p.wait("act", t)
                i = nc.scalar.activation(rowB[:, :], rowB[:, :], AF.Exp, scale=-0.5)
                rt = p.inc(i, "sACT")
                for tq in range(N_TQB):
                    p.wait("pe", rt, tok.get(f"{tg}bcq_free"))
                    i = nc.tensor.matmul(bcp[:, :], c_orow[:, :],
                                         rowB[0:1, tq * 512:(tq + 1) * 512],
                                         start=True, stop=True)
                    mt = p.inc(i, "sPE")
                    p.wait("act", mt)
                    i = nc.scalar.copy(al_bc[:, tq * 512:(tq + 1) * 512], bcp[:, :])
                    tok[f"{tg}bcq_free"] = p.inc(i, "sACT")
                rbc_tok = tok[f"{tg}bcq_free"]
                ct_tok = []
                for j in range(2):
                    p.wait("dve", rbc_tok)
                    i = nc.vector.scalar_tensor_tensor(
                        ctil[j][:, :], ctil[j][:, :], pp[:, PP_KVLN + j:PP_KVLN + j + 1],
                        al_bc[:, :], ALU.mult, ALU.mult)
                    ct_tok.append(p.inc(i, "sDVE"))
                krw = kvw[0]
                krl = wload(krw[:, 0:NC_D * 64],
                            kvar_b[:, :, :].rearrange("k p i -> p k i"),
                            NC_D * 64, deps=(tok[f"{tg}kvmm_{2}"],))
                for tq in range(N_TQB):
                    bank = psA[tq % 2]
                    p.wait("pe", krl, tok.get(f"{tg}kre_{tq-2}"))
                    p.guard(bank[0:64, :], krw[:, 0:64], y[0][:, tq * 512:(tq + 1) * 512])
                    for k in range(NC_D):
                        i = nc.tensor.matmul(bank[0:64, :], krw[:, k * 64:(k + 1) * 64],
                                             y[k][:, tq * 512:(tq + 1) * 512],
                                             start=(k == 0), stop=(k == NC_D - 1))
                    mt = p.inc(i, "sPE")
                    if tq == N_TQB - 1:
                        tok[f"{tg}krmm"] = mt
                    p.wait("act", mt)
                    i = nc.scalar.copy(ksc[:, tq * 512:(tq + 1) * 512], bank[0:64, :])
                    tok[f"{tg}kre_{tq}"] = p.inc(i, "sACT")
                krot_tok = tok[f"{tg}kre_{N_TQB-1}"]
                p.wait("pool", t1, t2, krot_tok, tok[f"{tg}kve_{2 * N_TQB + N_TQB - 1}"])
                i = nc.gpsimd.tensor_mul(kro[:, :], kpe[:, :], cosk[:, :])
                ta = p.inc(i, "sPOOL")
                i = nc.gpsimd.tensor_mul(ksc[:, :], ksc[:, :], sink[:, :])
                tb = p.inc(i, "sPOOL")
                p.wait("pool", ta, tb)
                i = nc.gpsimd.tensor_add(kro[:, :], kro[:, :], ksc[:, :])
                kro_tok = p.inc(i, "sPOOL")
                pk1 = p.dma("sp", ccs[0:128, :], ctil[0][:, :], "sS", deps=(ct_tok[0],))
                pk2 = p.dma("sp", ccs[128:256, :], ctil[1][:, :], "sS", deps=(ct_tok[1],))
                pk3 = p.dma("sp", ccs[256:320, :], kro[:, :], "sS", deps=(kro_tok,))
                p.wait("pool", pk1, pk2, pk3)
                i = nc.gpsimd.collective_compute(
                    "AllGather", ALU.bypass,
                    replica_groups=[[0, 1], [2, 3], [4, 5], [6, 7]],
                    ins=[ccs.opt()], outs=[ccd.opt()])
                cc_tok = p.inc(i, "sCC")

                for ic in range(NC_QR):
                    wb = kvw[ic % 2]
                    wlt = wload(wb[:, :], qa_b[ic, :, :, :].rearrange("p k i -> p (k i)"),
                                NC_D * 128, deps=(tok.get(f"{tg}qamm_{ic-2}"),
                                                  tok.get(f"{tg}krmm") if ic == 0 else None,))
                    for tq in range(N_TQB):
                        un = ic * N_TQB + tq
                        bank = psA[un % 2]
                        p.wait("pe", wlt, tok.get(f"{tg}qae_{un-2}"), cc_tok)
                        p.guard(bank[:, :], wb[:, 0:128], y[0][:, tq * 512:(tq + 1) * 512])
                        for k in range(NC_D):
                            i = nc.tensor.matmul(bank[:, :], wb[:, k * 128:(k + 1) * 128],
                                                 y[k][:, tq * 512:(tq + 1) * 512],
                                                 start=(k == 0), stop=(k == NC_D - 1))
                        mt = p.inc(i, "sPE")
                        if tq == N_TQB - 1:
                            tok[f"{tg}qamm_{ic}"] = mt
                        p.wait("act", mt)
                        i = nc.scalar.copy(qtil[ic][:, tq * 512:(tq + 1) * 512], bank[:, :])
                        tok[f"{tg}qae_{un}"] = p.inc(i, "sACT")
                for j in range(NC_QR):
                    p.wait("act", tok[f"{tg}qae_{j * N_TQB + N_TQB - 1}"], csqmm)
                    i = nc.scalar.activation(csq[:, :], qtil[j][:, :], AF.Square)
                    sqt = p.inc(i, "sACT")
                    p.wait("pe", sqt)
                    for tq in range(N_TQB):
                        i = nc.tensor.matmul(psR[0:1, tq * 512:(tq + 1) * 512],
                                             c_ones[:, :],
                                             csq[:, tq * 512:(tq + 1) * 512],
                                             start=(j == 0), stop=(j == NC_QR - 1))
                    qsqmm = p.inc(i, "sPE")
                p.wait("dve", qsqmm, *ct_tok)
                rowS = al_bc[0:1, :]
                i = nc.vector.tensor_scalar_mul(rowS, psR[0:1, :], 1.0 / 682.0)
                t = p.inc(i, "sDVE")
                p.wait("dve", t)
                i = nc.vector.tensor_scalar_add(rowS, rowS, 1e-6)
                t = p.inc(i, "sDVE")
                p.wait("act", t)
                i = nc.scalar.activation(rowB[:, :], rowS, AF.Ln)
                t = p.inc(i, "sACT")
                p.wait("act", t)
                i = nc.scalar.activation(rowB[:, :], rowB[:, :], AF.Exp, scale=-0.5)
                rt = p.inc(i, "sACT")
                for tq in range(N_TQB):
                    p.wait("pe", rt, tok.get(f"{tg}bcq_free"), rbc_tok, *ct_tok)
                    i = nc.tensor.matmul(bcp[:, :], c_orow[:, :],
                                         rowB[0:1, tq * 512:(tq + 1) * 512],
                                         start=True, stop=True)
                    mt = p.inc(i, "sPE")
                    p.wait("act", mt)
                    i = nc.scalar.copy(al_bc[:, tq * 512:(tq + 1) * 512], bcp[:, :])
                    tok[f"{tg}bcq_free"] = p.inc(i, "sACT")
                qr_bc = tok[f"{tg}bcq_free"]
                qt_tok = []
                for j in range(NC_QR):
                    p.wait("dve", qr_bc)
                    i = nc.vector.scalar_tensor_tensor(
                        qtil[j][:, :], qtil[j][:, :], pp[:, PP_QALN + j:PP_QALN + j + 1],
                        al_bc[:, :], ALU.mult, ALU.mult)
                    qt_tok.append(p.inc(i, "sDVE"))

            sA.close()
            p.barrier()
            ctf = [lsb(f"ctf{j}", [128, 2 * T], BF16) for j in range(2)]
            gview = ccd.rearrange("(r f) t -> f r t", r=2)
            g1 = p.dma("sp", ctf[0][:, :], gview[0:128, :, :], "sX", deps=(cc_tok,))
            g2 = p.dma("sp", ctf[1][:, :], gview[128:256, :, :], "sX", deps=(cc_tok,))
            gather_tok = (g1, g2)

            outall = [lsb(f"oa{h}", [128, T], BF16) for h in range(H)]
            qbw = [lsb(f"qbw{i}", [128, 3 * NC_QR * 64], BF16) for i in range(2)]
            vbw = [lsb(f"vbw{i}", [128, 256], BF16) for i in range(2)]
            kbw = [lsb(f"kbw{i}", [128, 128], BF16) for i in range(2)]
            sH = ExitStack()
            lsh = lambda name, shape, dt: sH.enter_context(
                nc.sbuf_tensor(name + tg, shape, dt))
            qh = lsh("qh", [128, T], BF16)
            qpe = lsh("qpe", [64, T], BF16)
            qrt = lsh("qrt", [64, T], BF16)
            kh = lsh("kh", [128, 2 * T], BF16)
            vtm = lsh("vtm", [128, 2 * T], BF16)
            pt = [lsh(f"pt{i}", [128, 512], BF16) for i in range(2)]
            cmx = lsh("cmx", [128, 1], F32)
            cmx2 = lsh("cmx2", [128, 1], F32)
            bias_pp = lsh("bias_pp", [128, 1], F32)
            rca = lsh("rca", [1, 512], F32)
            rbc_t = lsh("rbc_t", [128, 512], F32)
            NKB = 2 * T // 128
            with ExitStack() as pl:
                psS = [pl.enter_context(nc.psum_tensor(f"psS{tg}{i}", [128, 512], F32))
                       for i in range(2)]
                psO = pl.enter_context(nc.psum_tensor(f"psO{tg}", [128, 512], F32))
                psSm = pl.enter_context(nc.psum_tensor(f"psSm{tg}", [1, 512], F32))
                psH = [pl.enter_context(nc.psum_tensor(f"psH{tg}{i}", [128, 512], F32))
                       for i in range(2)]
                for h in range(H):
                    hdone_prev = tok.get(f"{tg}sdone_{h-1}")
                    qw = qbw[h % 2]
                    qwl = wload(qw[:, :],
                                qb_b[h, :, :, :, :].rearrange("s k p i -> p s k i"),
                                3 * NC_QR * 64, deps=(tok.get(f"{tg}qhmm_{h-2}"),))
                    vw = vbw[h % 2]
                    vwl = wload(vw[:, :], kvb_v[h, :, :, :].rearrange("p k i -> p (k i)"),
                                256, deps=(tok.get(f"{tg}vmm_{h-2}"),))
                    kw = kbw[h % 2]
                    kwl = wload(kw[:, :], kvb_k[h, :, :, :].rearrange("p k i -> p (k i)"),
                                128, deps=(tok.get(f"{tg}knmm_{h-2}"),))
                    for tq in range(N_TQB):
                        b0 = psH[tq % 2]
                        b1 = psH[(tq + 1) % 2]
                        p.wait("pe", qwl, *qt_tok)
                        p.wait("pe", tok.get(f"{tg}qhe_{h}_{tq-1}"), hdone_prev)
                        p.guard(b0[0:64, :], qw[:, 0:64],
                                qtil[0][:, tq * 512:(tq + 1) * 512])
                        for k in range(NC_QR):
                            i = nc.tensor.matmul(
                                b0[0:64, :], qw[:, k * 64:(k + 1) * 64],
                                qtil[k][:, tq * 512:(tq + 1) * 512],
                                start=(k == 0), stop=(k == NC_QR - 1))
                        for k in range(NC_QR):
                            i = nc.tensor.matmul(
                                b1[0:64, :], qw[:, (NC_QR + k) * 64:(NC_QR + k + 1) * 64],
                                qtil[k][:, tq * 512:(tq + 1) * 512],
                                start=(k == 0), stop=(k == NC_QR - 1))
                        for k in range(NC_QR):
                            i = nc.tensor.matmul(
                                psO[0:64, :], qw[:, (2 * NC_QR + k) * 64:(2 * NC_QR + k + 1) * 64],
                                qtil[k][:, tq * 512:(tq + 1) * 512],
                                start=(k == 0), stop=(k == NC_QR - 1))
                        mt = p.inc(i, "sPE")
                        if tq == N_TQB - 1:
                            tok[f"{tg}qhmm_{h}"] = mt
                        p.wait("act", mt, tok.get(f"{tg}qrope_{h-1}"))
                        i = nc.scalar.copy(qh[0:64, tq * 512:(tq + 1) * 512], b0[0:64, :])
                        e1 = p.inc(i, "sACT")
                        p.wait("act", e1)
                        i = nc.scalar.copy(qpe[:, tq * 512:(tq + 1) * 512], b1[0:64, :])
                        e2 = p.inc(i, "sACT")
                        p.wait("act", e2)
                        i = nc.scalar.copy(qrt[:, tq * 512:(tq + 1) * 512], psO[0:64, :])
                        tok[f"{tg}qhe_{h}_{tq}"] = p.inc(i, "sACT")
                    qpe_tok = tok[f"{tg}qhe_{h}_{N_TQB-1}"]
                    p.wait("pool", qpe_tok, t1, t2, tok.get(f"{tg}qhdma_{h-1}"))
                    i = nc.gpsimd.tensor_mul(qpe[:, :], qpe[:, :], cosk[:, :])
                    ra = p.inc(i, "sPOOL")
                    i = nc.gpsimd.tensor_mul(qrt[:, :], qrt[:, :], sink[:, :])
                    rb = p.inc(i, "sPOOL")
                    p.wait("pool", ra, rb)
                    i = nc.gpsimd.tensor_add(qpe[:, :], qpe[:, :], qrt[:, :])
                    qrope0 = p.inc(i, "sPOOL")
                    qdma = p.dma("sp", qh[64:128, :], qpe[:, :], "sX", deps=(qrope0,))
                    qrope_tok = qdma
                    tok[f"{tg}qrope_{h}"] = qrope0
                    tok[f"{tg}qhdma_{h}"] = qdma
                    for nb in range(8):
                        bank = psH[nb % 2]
                        p.wait("pe", kwl, *gather_tok)
                        p.wait("pe", tok.get(f"{tg}kne_{h}_{nb-2}"), hdone_prev,
                               tok[f"{tg}qhe_{h}_{N_TQB-1}"])
                        p.guard(bank[0:64, :], kw[:, 0:64],
                                ctf[0][:, nb * 512:(nb + 1) * 512])
                        for kc in range(2):
                            i = nc.tensor.matmul(bank[0:64, :],
                                                 kw[:, kc * 64:(kc + 1) * 64],
                                                 ctf[kc][:, nb * 512:(nb + 1) * 512],
                                                 start=(kc == 0), stop=(kc == 1))
                        mt = p.inc(i, "sPE")
                        if nb == 7:
                            tok[f"{tg}knmm_{h}"] = mt
                        p.wait("act", mt)
                        i = nc.scalar.copy(kh[0:64, nb * 512:(nb + 1) * 512],
                                           bank[0:64, :])
                        tok[f"{tg}kne_{h}_{nb}"] = p.inc(i, "sACT")
                    kh0_tok = tok[f"{tg}kne_{h}_{7}"]
                    if h == 0:
                        tok[f"{tg}kpecp"] = p.dma("sp", kh[64:128, :],
                                                  gview[256:320, :, :], "sX",
                                                  deps=(cc_tok,))
                    kh_tok = (kh0_tok, tok[f"{tg}kpecp"])
                    for vb in range(8):
                        bank = psH[vb % 2]
                        p.wait("pe", vwl, *gather_tok)
                        p.wait("pe", tok.get(f"{tg}ve_{h}_{vb-2}"), hdone_prev,
                               tok[f"{tg}kne_{h}_{7}"])
                        p.guard(bank[:, 0:128], ctf[0][:, vb * 512:vb * 512 + 128],
                                vw[:, 0:128])
                        for j4 in range(4):
                            tkb = vb * 4 + j4
                            for kc in range(2):
                                i = nc.tensor.matmul(
                                    bank[:, j4 * 128:(j4 + 1) * 128],
                                    ctf[kc][:, tkb * 128:(tkb + 1) * 128],
                                    vw[:, kc * 128:(kc + 1) * 128],
                                    start=(kc == 0), stop=(kc == 1))
                        mt = p.inc(i, "sPE")
                        if vb == 7:
                            tok[f"{tg}vmm_{h}"] = mt
                        p.wait("act", mt)
                        i = nc.scalar.copy(vtm[:, vb * 512:(vb + 1) * 512], bank[:, :])
                        tok[f"{tg}ve_{h}_{vb}"] = p.inc(i, "sACT")
                    v_tok = tok[f"{tg}ve_{h}_{7}"]
                    for tq in range(N_TQB):
                        bank = psS[tq % 2]
                        p.wait("pe", *kh_tok)
                        p.wait("pe", qrope_tok, qpe_tok)
                        if tq >= 2:
                            p.wait("pe", (f"sDVE", 0) if False else tok[f"{tg}cm_{h}_{tq-2}"])
                        i = nc.tensor.matmul(bank[:, :], kh[:, 0:128],
                                             qh[:, tq * 512:(tq + 1) * 512],
                                             start=True, stop=True)
                        mt = p.inc(i, "sPE")
                        p.wait("dve", mt)
                        if tq == 0:
                            i = nc.vector.reduce_max(cmx[:, :], bank[:, :], axis=AX.X)
                        else:
                            i = nc.vector.reduce_max(cmx2[:, :], bank[:, :], axis=AX.X)
                        t = p.inc(i, "sDVE")
                        if tq > 0:
                            p.wait("dve", t)
                            i = nc.vector.tensor_max(cmx[:, :], cmx[:, :], cmx2[:, :])
                            t = p.inc(i, "sDVE")
                        tok[f"{tg}cm_{h}_{tq}"] = t
                    cm_tok = tok[f"{tg}cm_{h}_{N_TQB-1}"]
                    p.wait("pe", cm_tok)
                    i = nc.tensor.transpose(psSm[0:1, 0:128], cmx[:, 0:1], c_id[:, :])
                    tr_t = p.inc(i, "sPE")
                    p.wait("dve", tr_t)
                    i = nc.vector.reduce_max(rca[0:1, 0:1], psSm[0:1, 0:128], axis=AX.X)
                    t = p.inc(i, "sDVE")
                    p.wait("dve", t)
                    i = nc.vector.tensor_scalar_mul(rca[0:1, 1:2], rca[0:1, 0:1],
                                                    -SCALE_ATT)
                    t = p.inc(i, "sDVE")
                    p.wait("pe", t)
                    i = nc.tensor.matmul(psO[:, 0:1], c_orow[:, :], rca[0:1, 1:2],
                                         start=True, stop=True)
                    mt = p.inc(i, "sPE")
                    p.wait("act", mt)
                    i = nc.scalar.copy(bias_pp[:, :], psO[:, 0:1])
                    bias_tok = p.inc(i, "sACT")

                    for tqq in range(N_TQB):
                        for kb in range(NKB):
                            bank = psS[kb % 2]
                            p.wait("pe", *kh_tok)
                            p.wait("pe", qrope_tok, bias_tok,
                                   tok.get(f"{tg}exp_{h}_{tqq}_{kb-2}"))
                            if kb == 0:
                                p.wait("pe", tok.get(f"{tg}oev_{h}_{tqq-1}"),
                                       tok.get(f"{tg}oev_{h-1}_{N_TQB-1}"))
                            i = nc.tensor.matmul(bank[:, :], kh[:, kb * 128:(kb + 1) * 128],
                                                 qh[:, tqq * 512:(tqq + 1) * 512],
                                                 start=True, stop=True)
                            smt = p.inc(i, "sPE")
                            pb = pt[kb % 2]
                            p.wait("act", smt, bias_tok,
                                   tok.get(f"{tg}avmm_{h}_{tqq}_{kb-2}"))
                            i = nc.scalar.activation(pb[:, :], bank[:, :], AF.Exp,
                                                     bias=bias_pp[:, 0:1], scale=SCALE_ATT)
                            ept = p.inc(i, "sACT")
                            tok[f"{tg}exp_{h}_{tqq}_{kb}"] = ept
                            p.wait("pe", ept, v_tok)
                            if kb == 0:
                                p.guard(psO[:, :], vtm[:, 0:128], pb[:, :])
                            i = nc.tensor.matmul(psO[:, :], vtm[:, kb * 128:(kb + 1) * 128],
                                                 pb[:, :], start=(kb == 0),
                                                 stop=(kb == NKB - 1))
                            i = nc.tensor.matmul(psSm[0:1, :], c_ones[:, :], pb[:, :],
                                                 start=(kb == 0), stop=(kb == NKB - 1))
                            amt = p.inc(i, "sPE")
                            tok[f"{tg}avmm_{h}_{tqq}_{kb}"] = amt
                        p.wait("dve", tok[f"{tg}avmm_{h}_{tqq}_{NKB-1}"])
                        i = nc.vector.reciprocal(rca[0:1, :], psSm[0:1, :])
                        rt_ = p.inc(i, "sDVE")
                        p.wait("pe", rt_)
                        i = nc.tensor.matmul(psS[0][:, :], c_orow[:, :], rca[0:1, :],
                                             start=True, stop=True)
                        bmt = p.inc(i, "sPE")
                        p.wait("act", bmt)
                        i = nc.scalar.copy(rbc_t[:, :], psS[0][:, :])
                        rbt = p.inc(i, "sACT")
                        p.wait("dve", rbt)
                        i = nc.vector.tensor_mul(outall[h][:, tqq * 512:(tqq + 1) * 512],
                                                 psO[:, :], rbc_t[:, :])
                        tok[f"{tg}oev_{h}_{tqq}"] = p.inc(i, "sDVE")
                    tok[f"{tg}sdone_{h}"] = tok[f"{tg}oev_{h}_{N_TQB-1}"]

                p.barrier()
                sH.close()
                obw = [lsb(f"obw{i}", [128, H * 128], BF16) for i in range(2)]
                for r in range(NC_D):
                    ow = obw[r % 2]
                    owl = wload(ow[:, :], o_b[r, :, :, :].rearrange("h p d -> p h d"),
                                H * 128, deps=(tok.get(f"{tg}omm_{r-2}"),))
                    for tq in range(N_TQB):
                        un = r * N_TQB + tq
                        bank = psS[un % 2]
                        p.wait("pe", owl, tok[f"{tg}sdone_{H-1}"],
                               tok.get(f"{tg}oe_{un-2}"))
                        p.guard(bank[:, :], ow[:, 0:128],
                                outall[0][:, tq * 512:(tq + 1) * 512])
                        for hh2 in range(H):
                            i = nc.tensor.matmul(bank[:, :],
                                                 ow[:, hh2 * 128:(hh2 + 1) * 128],
                                                 outall[hh2][:, tq * 512:(tq + 1) * 512],
                                                 start=(hh2 == 0), stop=(hh2 == H - 1))
                        mt = p.inc(i, "sPE")
                        if tq == N_TQB - 1:
                            tok[f"{tg}omm_{r}"] = mt
                        xr = xres[un % 2]
                        rlt = p.dma("sp", xr[:, :], x1d[r, :, tq * 512:(tq + 1) * 512],
                                    "sX", deps=(tok.get(f"{tg}oe_{un-2}"),))
                        ob_ = oout[un % 2]
                        p.wait("dve", mt, rlt, tok.get(f"{tg}ost_{un-2}"))
                        i = nc.vector.tensor_add(ob_[:, :], bank[:, :], xr[:, :])
                        et = p.inc(i, "sDVE")
                        tok[f"{tg}oe_{un}"] = et
                        tok[f"{tg}ost_{un}"] = p.dma(
                            "sp", x2d[r, :, tq * 512:(tq + 1) * 512], ob_[:, :],
                            "sS", deps=(et,))

    emit_scale_prep()
    if PHASE_LIMIT is None or PHASE_LIMIT >= 2:
        emit_ffn(0, xT, xT, x1d if (PHASE_LIMIT is None or PHASE_LIMIT >= 3) else yout)
    if PHASE_LIMIT is None or PHASE_LIMIT >= 3:
        emit_attention()
    if PHASE_LIMIT is None:
        emit_ffn(1, x2d, x2d, yout)
    p.barrier()
    if DEBUG_Z:
        db = p.dma("sp", zdbg[:, :, :], z1d[:, :, :], "sS")
    if "sS" in p.cnt:
        p.wait("sp", ("sS", p.cnt["sS"]))
    stack.close()
    return nc


def _host_layout(inp):
    f32 = np.float32
    L = {}
    x = np.asarray(inp["L0_attn_x"], f32)
    shards = []
    for c in range(8):
        b, hh = c // 2, c % 2
        xs = np.ascontiguousarray(x[b, hh * T:(hh + 1) * T, :].T)
        shards.append(np.ascontiguousarray(xs.reshape(NC_D, 128, T)))
    W_up = np.asarray(inp["L0_ffn_W_up"], f32)
    W_gt = np.asarray(inp["L0_ffn_W_gate"], f32)
    W_dn = np.asarray(inp["L0_ffn_W_down"], f32)
    L["wup_b"] = np.ascontiguousarray(
        W_up.T.reshape(NC_D, 128, NC_I, 128).transpose(2, 1, 0, 3))
    L["wgt_b"] = np.ascontiguousarray(
        W_gt.T.reshape(NC_D, 128, NC_I, 128).transpose(2, 1, 0, 3))
    L["wdn_b"] = np.ascontiguousarray(
        W_dn.T.reshape(64, 64, 2, D).transpose(1, 0, 2, 3).reshape(NC_I, 128, D))
    qa = np.asarray(inp["L0_attn_q_a_w"], f32)
    qa_p = np.zeros((QRP, D), f32)
    qa_p[:QR] = qa
    L["qa_b"] = np.ascontiguousarray(
        qa_p.T.reshape(NC_D, 128, NC_QR, 128).transpose(2, 1, 0, 3))
    qb = np.asarray(inp["L0_attn_q_b_w"], f32)
    perm = np.arange(2048)
    for hh in range(H):
        base = hh * 128 + 64
        idx = np.arange(64).reshape(32, 2).T.reshape(64)
        perm[base:base + 64] = base + idx
    qb_perm = qb[perm]
    qb_p = np.zeros((QRP, D), f32)
    qb_p[:QR] = qb_perm.T
    rotp = np.concatenate([np.arange(32, 64), np.arange(0, 32)])
    qb_sets = np.zeros((H, 3, NC_QR, 128, 64), f32)
    for hh in range(H):
        cn = qb_p[:, hh * 128: hh * 128 + 64]
        cp = qb_p[:, hh * 128 + 64: hh * 128 + 128]
        cr = cp[:, rotp]
        for s3, cc in enumerate((cn, cp, cr)):
            qb_sets[hh, s3] = cc.reshape(NC_QR, 128, 64)
    L["qb_b"] = np.ascontiguousarray(qb_sets)
    kva = np.asarray(inp["L0_attn_kv_a_w"], f32)
    kperm = np.arange(320)
    idx = np.arange(64).reshape(32, 2).T.reshape(64)
    kperm[256:320] = 256 + idx
    kva_perm = kva[kperm]
    kva_p = np.zeros((KVAP, D), f32)
    kva_p[:320] = kva_perm
    L["kva_b"] = np.ascontiguousarray(
        kva_p.T.reshape(NC_D, 128, 3, 128).transpose(2, 1, 0, 3))
    rotp = np.concatenate([np.arange(32, 64), np.arange(0, 32)])
    kva_pe = kva_p.T[:, 256:320]
    L["kvar_b"] = np.ascontiguousarray(
        kva_pe[:, rotp].reshape(NC_D, 128, 64))
    kvb = np.asarray(inp["L0_attn_kv_b_w"], f32)
    kvbT = kvb.T
    vb = np.zeros((H, 128, 2, 128), f32)
    kb = np.zeros((H, 128, 2, 64), f32)
    for hh in range(H):
        vcols = kvbT[:, hh * 192 + 64: hh * 192 + 192]
        vb[hh] = vcols.reshape(2, 128, 128).transpose(1, 0, 2)
        kcols = kvbT[:, hh * 192: hh * 192 + 64]
        kb[hh] = kcols.reshape(2, 128, 64).transpose(1, 0, 2)
    L["kvb_v"] = np.ascontiguousarray(vb)
    L["kvb_k"] = np.ascontiguousarray(kb)
    ow = np.asarray(inp["L0_attn_o_w"], f32)
    L["o_b"] = np.ascontiguousarray(
        ow.T.reshape(H, 128, NC_D, 128).transpose(2, 0, 1, 3))
    L["su_ug"] = np.ascontiguousarray(np.asarray(inp["L0_ffn_SU_ug"], f32).reshape(NC_D, 128))
    L["su_dp"] = np.ascontiguousarray(np.asarray(inp["L0_ffn_SU_dp"], f32).reshape(NC_I, 128))
    wsug = np.asarray(inp["L0_ffn_Wscale_ug"], f32)
    L["ws_up"] = np.ascontiguousarray(wsug[:INTER].reshape(NC_I, 128))
    L["ws_gt"] = np.ascontiguousarray(wsug[INTER:].reshape(NC_I, 128))
    L["ws_dp"] = np.ascontiguousarray(np.asarray(inp["L0_ffn_Wscale_dp"], f32).reshape(NC_D, 128))
    qal = np.zeros((QRP,), f32)
    qal[:QR] = np.asarray(inp["L0_attn_q_a_ln"], f32)
    L["qaln"] = np.ascontiguousarray(qal.reshape(NC_QR, 128))
    L["kvln"] = np.ascontiguousarray(np.asarray(inp["L0_attn_kv_a_ln"], f32).reshape(2, 128))
    Hm = np.array([[1.0]], f32)
    while Hm.shape[0] < 128:
        Hm = np.block([[Hm, Hm], [Hm, -Hm]])
    L["h128c"] = Hm.astype(ml_dtypes.bfloat16)
    Hm64 = np.array([[1.0]], f32)
    while Hm64.shape[0] < 64:
        Hm64 = np.block([[Hm64, Hm64], [Hm64, -Hm64]])
    L["hd64c"] = np.kron(Hm64, np.eye(2, dtype=f32)).astype(ml_dtypes.bfloat16)
    L["onescol"] = np.ones((128, 1), ml_dtypes.bfloat16)
    L["onesrow"] = np.ones((1, 128), f32)
    L["identc"] = np.eye(128, dtype=f32)
    inv = 1.0 / THETA ** (np.arange(0, ROPE, 2, dtype=np.float64) / ROPE)
    sgn = np.concatenate([-np.ones(32), np.ones(32)])
    cos_t, sin_t = [], []
    for c in range(8):
        hh = c % 2
        pos = np.arange(hh * T, (hh + 1) * T, dtype=np.float64)
        fr = pos[None, :] * inv[:, None]
        cc = np.concatenate([np.cos(fr), np.cos(fr)], 0)
        ss = np.concatenate([np.sin(fr), np.sin(fr)], 0) * sgn[:, None]
        cos_t.append(np.ascontiguousarray(cc.astype(ml_dtypes.bfloat16)))
        sin_t.append(np.ascontiguousarray(ss.astype(ml_dtypes.bfloat16)))
    return L, shards, cos_t, sin_t


def kernel(**inputs):
    L, shards, cos_t, sin_t = _host_layout(inputs)
    nc = build_program()
    in_maps = []
    for c in range(8):
        m = dict(L)
        m["xT"] = shards[c]
        m["cosq"] = cos_t[c]
        m["sinq"] = sin_t[c]
        in_maps.append(m)
    res = run_bass_kernel_spmd(nc, in_maps, core_ids=list(range(8)))
    out = np.empty((B, S, D), np.float32)
    for c in range(8):
        b, hh = c // 2, c % 2
        yc = res.results[c]["yout"].reshape(D, T)
        out[b, hh * T:(hh + 1) * T, :] = yc.T
    return out
